# version 23
# speedup vs baseline: 1.0425x; 1.0298x over previous
import numpy as np
import ml_dtypes
import concourse.bass as bass
import concourse.mybir as mybir
from concourse.bass_utils import run_bass_kernel_spmd

F32 = mybir.dt.float32
BF16 = mybir.dt.bfloat16
I32 = mybir.dt.int32
AF = mybir.ActivationFunctionType
ALU = mybir.AluOpType
AX = mybir.AxisListType

NCORES = 8
E = 2048
TPC = 1024
NH, HD, NS, NG = 64, 64, 64, 8
INTER_A = 4096
CONV_DIM = 5120
PROJ = 9280
CHUNK_A = 256
INTER_F = 5504
LN_EPS = 1e-5
RMS_EPS = 1e-6

D_TILES = E // 128
IA_TILES = INTER_A // 128
O_TILES_P1 = 41
BF = ml_dtypes.bfloat16



def tile_lhsT(w, ktiles, mtile=128):
    K, M = w.shape
    assert K == ktiles * 128
    Mt = (M + mtile - 1) // mtile
    wp = np.zeros((K, Mt * mtile), np.float32)
    wp[:, :M] = np.asarray(w, np.float32)
    a = wp.reshape(ktiles, 128, Mt, mtile).transpose(2, 1, 0, 3)
    return np.ascontiguousarray(a.reshape(Mt, 128, ktiles * mtile))


def selpack_pair():
    a = np.zeros((64, 32 * 128), BF)
    for j in range(32):
        a[2 * j, j * 128:j * 128 + 64] = 1
        a[2 * j + 1, j * 128 + 64:(j + 1) * 128] = 1
    return a


def selpack_single():
    a = np.zeros((64, 64 * 128), BF)
    for h in range(64):
        a[h, h * 128:(h + 1) * 128] = 1
    return a


def trimask_np(st):
    s = np.arange(128)[:, None] + 128 * st
    l = np.arange(256)[None, :]
    return (l >= s).astype(BF)


SEMNAMES = ["ld", "gl", "pe", "dv", "ac", "gp", "st"]

_SERIAL_OPS = {
    "memset", "memzero", "tensor_copy", "tensor_tensor", "tensor_add",
    "tensor_sub", "tensor_mul", "tensor_max", "tensor_scalar",
    "tensor_single_scalar", "tensor_scalar_add", "tensor_scalar_sub",
    "tensor_scalar_mul", "tensor_scalar_max", "tensor_scalar_min",
    "tensor_relu", "tensor_reduce", "reduce_max", "reduce_sum",
    "scalar_tensor_tensor", "tensor_tensor_scan", "reciprocal", "select",
    "copy_predicated", "transpose", "activation", "copy", "mul", "add",
    "sign", "square", "sqrt",
}


class SerialEng:

    def __init__(self, eng, auto_sem):
        self._e = eng
        self._auto = auto_sem
        self._n = 0
        self._pend = None
        self._pw = None

    def _flush(self):
        if self._pend is not None:
            self._n += 1
            self._pend.then_inc(self._auto, 1)
            self._pw = (self._auto, self._n)
            self._pend = None

    def note_milestone(self, inst, sem_handle, value):
        if inst is self._pend:
            self._pw = (sem_handle, value)
            self._pend = None

    def __getattr__(self, name):
        attr = getattr(self._e, name)
        if name not in _SERIAL_OPS:
            return attr

        def wrapped(*a, **kw):
            self._flush()
            if self._pw is not None:
                self._e.wait_ge(self._pw[0], self._pw[1])
                self._pw = None
            inst = attr(*a, **kw)
            self._pend = inst
            return inst
        return wrapped


class Ctx:

    def __init__(self, ename, eng, sems, spool):
        self.ename = ename
        self.eng = eng
        self.sems = sems
        self.c = {n: 0 for n in SEMNAMES}
        self.marks = {}
        self.spool = spool
        self.streams = {}

    def _stream(self, name):
        st = self.streams.get(name)
        if st is None:
            idx = len(self.streams)
            assert idx < len(self.spool), "DMA stream semaphore pool exhausted"
            st = [idx, 0]
            self.streams[name] = st
        return st

    def dma(self, ename, stream, out, in_):
        st = self._stream(stream)
        if self.ename == ename:
            self.eng.dma_start(out=out, in_=in_).then_inc(self.spool[st[0]], 16)
        st[1] += 16

    def wait_stream(self, ename, stream):
        st = self._stream(stream)
        if st[1] > 0 and self.ename == ename:
            self.eng.wait_ge(self.spool[st[0]], st[1])

    def on(self, ename):
        return self.ename == ename

    def inc(self, inst, sem, n=1):
        self.c[sem] += n
        if inst is not None:
            inst.then_inc(self.sems[sem], n)
            if isinstance(self.eng, SerialEng):
                self.eng.note_milestone(inst, self.sems[sem], self.c[sem])
        return self.c[sem]

    def mark(self, res, sem):
        self.marks[res] = (sem, self.c[sem])

    def wait(self, ename, sem, val=None):
        if val is None:
            val = self.c[sem]
        if val > 0 and self.ename == ename:
            self.eng.wait_ge(self.sems[sem], val)

    def wait_mark(self, ename, res):
        m = self.marks.get(res)
        if m is not None and m[1] > 0 and self.ename == ename:
            self.eng.wait_ge(self.sems[m[0]], m[1])



def build_launch_a():
    nc = bass.Bass("TRN2", target_bir_lowering=False)

    xT = nc.dram_tensor("xT", [E, 1028], F32, kind="ExternalInput").ap()
    winT = nc.dram_tensor("winT", [73, 128, D_TILES * 128], BF16, kind="ExternalInput").ap()
    woutT = nc.dram_tensor("woutT", [16, 128, IA_TILES * 128], BF16, kind="ExternalInput").ap()
    convw = nc.dram_tensor("convw", [128, 160], F32, kind="ExternalInput").ap()
    convb = nc.dram_tensor("convb", [128, 40], F32, kind="ExternalInput").ap()
    normw = nc.dram_tensor("normw", [128, 32], F32, kind="ExternalInput").ap()
    hvec = nc.dram_tensor("hvec", [64, 2], F32, kind="ExternalInput").ap()
    dpair = nc.dram_tensor("dpair", [128, 32], F32, kind="ExternalInput").ap()
    lutbc = nc.dram_tensor("lutbc", [128, 16], F32, kind="ExternalInput").ap()
    pselb = nc.dram_tensor("pselb", [64, 32 * 128], BF16, kind="ExternalInput").ap()
    sselb = nc.dram_tensor("sselb", [64, 64 * 128], BF16, kind="ExternalInput").ap()
    tri0d = nc.dram_tensor("tri0", [128, 256], BF16, kind="ExternalInput").ap()
    tri1d = nc.dram_tensor("tri1", [128, 256], BF16, kind="ExternalInput").ap()
    identbd = nc.dram_tensor("identb", [128, 128], BF16, kind="ExternalInput").ap()
    identfd = nc.dram_tensor("identf", [128, 128], F32, kind="ExternalInput").ap()
    onescolbd = nc.dram_tensor("onescolb", [128, 1], BF16, kind="ExternalInput").ap()
    onesrowbd = nc.dram_tensor("onesrowb", [1, 128], BF16, kind="ExternalInput").ap()
    epsvd = nc.dram_tensor("epsv", [128, 8], F32, kind="ExternalInput").ap()
    gidx = nc.dram_tensor("gidx", [256, INTER_F], BF16, kind="ExternalInput").ap()
    uidx = nc.dram_tensor("uidx", [256, INTER_F], BF16, kind="ExternalInput").ap()
    didx = nc.dram_tensor("didx", [688, E], BF16, kind="ExternalInput").ap()

    x1T = nc.dram_tensor("x1T", [E, TPC], F32, kind="ExternalOutput").ap()
    x1bT = nc.dram_tensor("x1bT", [E, TPC], BF16, kind="ExternalOutput").ap()
    wg_sh = nc.dram_tensor("wg_sh", [256, INTER_F], BF16, kind="ExternalOutput").ap()
    wu_sh = nc.dram_tensor("wu_sh", [256, INTER_F], BF16, kind="ExternalOutput").ap()
    wd_sh = nc.dram_tensor("wd_sh", [688, E], BF16, kind="ExternalOutput").ap()

    from contextlib import ExitStack
    with ExitStack() as ctx:
        en = ctx.enter_context
        sb = lambda name, shape, dt: en(nc.sbuf_tensor(name, shape, dt))

        convw_s = sb("convw_s", [128, 160], F32)
        convb_s = sb("convb_s", [128, 40], F32)
        normw_s = sb("normw_s", [128, 32], F32)
        hvec_s = sb("hvec_s", [64, 2], F32)
        dpair_s = sb("dpair_s", [128, 32], F32)
        lutbc_s = sb("lutbc_s", [128, 16], F32)
        psel_s = sb("psel_s", [64, 32 * 128], BF16)
        ssel_s = sb("ssel_s", [64, 64 * 128], BF16)
        tri_s = [sb(f"tri{st}_s", [128, 256], BF16) for st in range(2)]
        identb_s = sb("identb_s", [128, 128], BF16)
        identf_s = sb("identf_s", [128, 128], F32)
        onescolb_s = sb("onescolb_s", [128, 1], BF16)
        onesrowb_s = sb("onesrowb_s", [1, 128], BF16)
        epsv_s = sb("epsv_s", [128, 8], F32)
        expa_s = sb("expa_s", [64, 1], F32)

        WMW = 512
        widxb2 = [sb(f"widxb{i}", [128, WMW], BF16) for i in range(2)]
        maskb2 = [sb(f"maskb{i}", [128, WMW], BF16) for i in range(2)]
        ones512 = sb("ones512", [128, WMW], BF16)
        idsc_s = sb("idsc_s", [128, 4096], BF16)
        wstg = [sb(f"wstg{i}", [128, WMW], BF16) for i in range(2)]
        ddall = sb("ddall", [128, 16], F32)
        dhf = sb("dhf", [128, 16], F32)
        dlof = sb("dlof", [128, 16], F32)

        xbf = [sb(f"xbf{i}", [128, 516], BF16) for i in range(2)]
        xres = [sb(f"xres{i}", [128, 512], F32) for i in range(2)]
        xln = [sb(f"xln{i}", [128, 516], BF16) for i in range(D_TILES)]
        hbc = [sb(f"hbc{i}", [128, 516], BF16) for i in range(40)]
        tails = sb("tails", [128, 160], BF16)
        cv1 = [sb(f"cv1_{i}", [128, 512], F32) for i in range(2)]
        dtr = sb("dtr", [64, 516], F32)
        dts1 = sb("dts1", [64, 512], F32)
        dts2 = sb("dts2", [64, 512], F32)
        dtb = sb("dtb", [64, 512], BF16)
        As_s = sb("As_s", [64, 512], F32)
        acum = sb("acum", [64, 512], F32)
        zer64 = sb("zer64", [64, 256], F32)
        decay = sb("decay", [64, 512], BF16)
        sdt = sb("sdt", [64, 512], BF16)
        atok = [sb(f"atok{st}", [128, 64], F32) for st in range(2)]
        ahi = sb("ahi", [64, 512], BF16)
        alo = sb("alo", [64, 512], BF16)
        alolo = sb("alolo", [64, 512], BF16)
        arr_ = sb("arr_", [64, 512], F32)
        hsdt1 = [sb(f"hsdt1_{p}", [128, 256], BF16) for p in range(2)]
        htokp = [[sb(f"htokp{p}_{st}", [128, 128], BF16) for st in range(2)]
                 for p in range(2)]
        gmall2 = sb("gmall2", [128, 8 * 512], BF16)
        dsc = [sb(f"dsc{p}", [128, 512], F32) for p in range(2)]
        esc = [sb(f"esc{p}", [128, 512], BF16) for p in range(2)]
        msc = [sb(f"msc{p}", [128, 512], BF16) for p in range(2)]
        pp_s = [sb(f"pp_s{p}", [128, 256], BF16) for p in range(2)]
        pp2_s = [sb(f"pp2_s{p}", [128, 256], BF16) for p in range(2)]
        stcol = [sb(f"stcol{p}", [128, 1], F32) for p in range(2)]
        yo_s = [sb(f"yo_s{p}", [128, 256], BF16) for p in range(2)]
        yt_s = [sb(f"yt_s{p}", [128, 256], BF16) for p in range(2)]
        y_sb = [sb(f"y{i}", [128, 512], BF16) for i in range(IA_TILES)]
        gsil = [sb(f"gsil{i}", [128, 512], BF16) for i in range(2)]
        sqt = [sb(f"sqt{i}", [128, 516], BF16) for i in range(2)]
        rowf = sb("rowf", [1, 1536], F32)
        rowb = sb("rowb", [1, 1024], BF16)
        rbc_s = sb("rbc_s", [128, 512], BF16)
        mbc_s = sb("mbc_s", [128, 512], BF16)
        rowhf = sb("rowhf", [1, 32], F32)
        rowhb = sb("rowhb", [1, 16], BF16)
        rbch_s = sb("rbch_s", [128, 8], BF16)
        mbch_s = sb("mbch_s", [128, 8], BF16)
        wslab = [sb(f"wslab{i}", [128, 2048], BF16) for i in range(2)]
        woslab = wslab
        x1bb = [sb(f"x1bb{i}", [128, 512], BF16) for i in range(1)]

        PS = [en(nc.psum_tensor(f"PS{i}", [128, 512], F32)) for i in range(6)]
        PT = [en(nc.psum_tensor(f"PT{i}", [128, 1024], BF16)) for i in range(2)]

        sems = {n: en(nc.semaphore("s_" + n)) for n in SEMNAMES}
        spool = [en(nc.semaphore(f"sp{i}")) for i in range(52)]
        auto = {n: en(nc.semaphore("a_" + n)) for n in ["dv", "ac", "gp"]}
        block = en(nc.Block())

        wm_slabs = []
        for odr, idr, rows, cols in [(wg_sh, gidx, 256, INTER_F),
                                     (wu_sh, uidx, 256, INTER_F),
                                     (wd_sh, didx, 688, E)]:
            r0 = 0
            while r0 < rows:
                npart = min(128, rows - r0)
                c0 = 0
                while c0 < cols:
                    w = min(WMW, cols - c0)
                    wm_slabs.append((odr, idr, r0, npart, c0, w))
                    c0 += w
                r0 += npart

        def emit(X: Ctx):
            eng = X.eng

            for dst, dsrc in [
                (convw_s[:], convw), (convb_s[:], convb), (normw_s[:], normw),
                (hvec_s[:], hvec), (dpair_s[:], dpair), (lutbc_s[:], lutbc),
                (psel_s[:], pselb), (ssel_s[:], sselb),
                (tri_s[0][:], tri0d), (tri_s[1][:], tri1d),
                (identb_s[:], identbd), (identf_s[:], identfd),
                (onescolb_s[:], onescolbd), (onesrowb_s[:], onesrowbd),
                (epsv_s[:], epsvd),
            ]:
                X.dma("sync", "c", dst, dsrc)
            for e_ in ["vector", "scalar", "tensor", "gpsimd"]:
                X.wait_stream(e_, "c")
            i1 = None
            if X.on("vector"):
                eng.memset(zer64[:], 0.0)
                eng.memset(ones512[:], 1.0)
                eng.tensor_copy(ddall[:, 0:1], lutbc_s[:, 0:1])
                eng.tensor_sub(ddall[:, 1:16], lutbc_s[:, 1:16], lutbc_s[:, 0:15])
                eng.tensor_copy(wstg[0][:, 0:16], ddall[:])
                eng.tensor_copy(dhf[:], wstg[0][:, 0:16])
                eng.tensor_sub(dlof[:], ddall[:], dhf[:])
                for k in range(16):
                    eng.tensor_scalar(idsc_s[:, 128 * k:128 * (k + 1)],
                                      identb_s[:], dhf[:, k:k + 1], None, ALU.mult)
                    i1 = eng.tensor_scalar(
                        idsc_s[:, 2048 + 128 * k:2048 + 128 * (k + 1)],
                        identb_s[:], dlof[:, k:k + 1], None, ALU.mult)
            X.inc(i1, "dv")
            X.mark(("idscw",), "dv")
            i1 = None
            if X.on("scalar"):
                i1 = eng.activation(expa_s[:], hvec_s[:, 1:2], AF.Exp)
            X.inc(i1, "ac")

            X.wait_mark("tensor", ("idscw",))
            mk = 0
            for i, (odr, idr, r0, npart, c0, w) in enumerate(wm_slabs):
                ib = i % 2
                bank = PS[4 + ib]
                X.wait_mark("sync", ("widxfree", ib))
                X.dma("sync", f"wi{ib}", widxb2[ib][0:npart, 0:w],
                      idr[r0:r0 + npart, c0:c0 + w])
                X.wait_mark("tensor", ("wbank", ib))
                if X.on("tensor"):
                    eng.matmul(bank[0:npart, 0:w], idsc_s[0:npart, 0:npart],
                               ones512[0:npart, 0:w], start=True, stop=False)
                    eng.matmul(bank[0:npart, 0:w],
                               idsc_s[0:npart, 2048:2048 + npart],
                               ones512[0:npart, 0:w], start=False, stop=False)
                X.wait_stream("vector", f"wi{ib}")
                for k in range(1, 16):
                    mb = maskb2[mk % 2]
                    X.wait_mark("vector", ("mfree", mk % 2))
                    i1 = None
                    if X.on("vector"):
                        i1 = eng.tensor_scalar(mb[0:npart, 0:w],
                                               widxb2[ib][0:npart, 0:w],
                                               float(k) - 0.5, None, ALU.is_ge)
                    X.inc(i1, "dv")
                    X.mark(("mrdy", mk % 2), "dv")
                    X.wait_mark("tensor", ("mrdy", mk % 2))
                    i1 = None
                    if X.on("tensor"):
                        eng.matmul(bank[0:npart, 0:w],
                                   idsc_s[0:npart, 128 * k:128 * k + npart],
                                   mb[0:npart, 0:w], start=False, stop=False)
                        i1 = eng.matmul(
                            bank[0:npart, 0:w],
                            idsc_s[0:npart, 2048 + 128 * k:2048 + 128 * k + npart],
                            mb[0:npart, 0:w], start=False, stop=(k == 15))
                    X.inc(i1, "pe")
                    X.mark(("mfree", mk % 2), "pe")
                    mk += 1
                X.mark(("widxfree", ib), "dv")
                X.mark(("wpsum", ib), "pe")
                X.wait_mark("scalar", ("wpsum", ib))
                X.wait_stream("scalar", f"wo{ib}")
                i1 = None
                if X.on("scalar"):
                    i1 = eng.activation(wstg[ib][0:npart, 0:w],
                                        bank[0:npart, 0:w], AF.Copy, bias=0.0)
                X.inc(i1, "ac")
                X.mark(("wbank", ib), "ac")
                X.wait_mark("gpsimd", ("wbank", ib))
                X.dma("gpsimd", f"wo{ib}", odr[r0:r0 + npart, c0:c0 + w],
                      wstg[ib][0:npart, 0:w])
            for b_ in (0, 1):
                if ("wbank", b_) in X.marks:
                    X.marks[("psfree", 4 + b_)] = X.marks[("wbank", b_)]

            for blk in range(2):
                emit_block(X, blk)
            return X

        def emit_ln(X, blk, main0, xw, halo, xc0):
            eng = X.eng
            for dt_ in range(D_TILES):
                bb = dt_ % 2
                X.wait_mark("gpsimd", ("xbffree", bb))
                X.dma("gpsimd", f"xa{bb}", xbf[bb][:, 0:xw],
                      xT[dt_ * 128:(dt_ + 1) * 128, xc0:xc0 + xw])
                X.wait_mark("scalar", ("sqfree", bb))
                X.wait_stream("scalar", f"xa{bb}")
                i1 = None
                if X.on("scalar"):
                    i1 = eng.activation(sqt[bb][:, 0:xw], xbf[bb][:, 0:xw], AF.Square)
                X.inc(i1, "ac")
                X.mark(("sqw", bb), "ac")
                X.wait_mark("tensor", ("sqw", bb))
                X.wait_stream("tensor", f"xa{bb}")
                if dt_ == 0:
                    for b_ in ([0, 1, 2, 3] if halo else [0, 1]):
                        X.wait_mark("tensor", ("psfree", b_))
                i1 = None
                if X.on("tensor"):
                    eng.matmul(PS[0][0:1, 0:512], onescolb_s[:],
                               xbf[bb][:, main0:main0 + 512],
                               start=(dt_ == 0), stop=(dt_ == D_TILES - 1))
                    i1 = eng.matmul(PS[1][0:1, 0:512], onescolb_s[:],
                                    sqt[bb][:, main0:main0 + 512],
                                    start=(dt_ == 0), stop=(dt_ == D_TILES - 1))
                    if halo:
                        eng.matmul(PS[2][0:1, 0:8], onescolb_s[:], xbf[bb][:, 0:8],
                                   start=(dt_ == 0), stop=(dt_ == D_TILES - 1))
                        i1 = eng.matmul(PS[3][0:1, 0:8], onescolb_s[:], sqt[bb][:, 0:8],
                                        start=(dt_ == 0), stop=(dt_ == D_TILES - 1))
                X.inc(i1, "pe")
                X.mark(("xbffree", bb), "pe")
                X.mark(("sqfree", bb), "pe")

            X.wait("vector", "pe")
            i1 = None
            if X.on("vector"):
                eng.tensor_scalar(rowf[0:1, 0:512], PS[0][0:1, 0:512], 1.0 / E, None, ALU.mult)
                eng.tensor_scalar(rowf[0:1, 512:1024], PS[1][0:1, 0:512], 1.0 / E, None, ALU.mult)
                eng.tensor_mul(rowf[0:1, 1024:1536], rowf[0:1, 0:512], rowf[0:1, 0:512])
                i1 = eng.tensor_sub(rowf[0:1, 1024:1536], rowf[0:1, 512:1024],
                                    rowf[0:1, 1024:1536])
                if halo:
                    i1
                    eng.tensor_scalar(rowhf[0:1, 0:8], PS[2][0:1, 0:8], 1.0 / E, None, ALU.mult)
                    eng.tensor_scalar(rowhf[0:1, 8:16], PS[3][0:1, 0:8], 1.0 / E, None, ALU.mult)
                    eng.tensor_mul(rowhf[0:1, 16:24], rowhf[0:1, 0:8], rowhf[0:1, 0:8])
                    i1 = eng.tensor_sub(rowhf[0:1, 16:24], rowhf[0:1, 8:16],
                                        rowhf[0:1, 16:24])
            X.inc(i1, "dv")
            X.mark(("psfree", 0), "dv")
            X.mark(("psfree", 1), "dv")
            if halo:
                X.mark(("psfree", 2), "dv")
                X.mark(("psfree", 3), "dv")
            X.wait("scalar", "dv")
            i1 = None
            if X.on("scalar"):
                eng.activation(rowf[0:1, 1024:1536], rowf[0:1, 1024:1536], AF.Ln,
                               bias=epsv_s[0:1, 0:1])
                i1 = eng.activation(rowf[0:1, 1024:1536], rowf[0:1, 1024:1536],
                                    AF.Exp, scale=-0.5)
                if halo:
                    i1
                    eng.activation(rowhf[0:1, 16:24], rowhf[0:1, 16:24], AF.Ln,
                                   bias=epsv_s[0:1, 0:1])
                    i1 = eng.activation(rowhf[0:1, 16:24], rowhf[0:1, 16:24],
                                        AF.Exp, scale=-0.5)
            X.inc(i1, "ac")
            X.wait("vector", "ac")
            i1 = None
            if X.on("vector"):
                eng.tensor_copy(rowb[0:1, 0:512], rowf[0:1, 1024:1536])
                eng.tensor_mul(rowf[0:1, 0:512], rowf[0:1, 0:512], rowf[0:1, 1024:1536])
                i1 = eng.tensor_copy(rowb[0:1, 512:1024], rowf[0:1, 0:512])
                if halo:
                    i1
                    eng.tensor_copy(rowhb[0:1, 0:8], rowhf[0:1, 16:24])
                    eng.tensor_mul(rowhf[0:1, 0:8], rowhf[0:1, 0:8], rowhf[0:1, 16:24])
                    i1 = eng.tensor_copy(rowhb[0:1, 8:16], rowhf[0:1, 0:8])
            X.inc(i1, "dv")

            X.wait("tensor", "dv")
            for b_ in ([0, 1, 2, 3] if halo else [0, 1]):
                X.wait_mark("tensor", ("psfree", b_))
            i1 = None
            if X.on("tensor"):
                eng.matmul(PS[0][:, 0:512], onesrowb_s[:], rowb[0:1, 0:512],
                           start=True, stop=True)
                i1 = eng.matmul(PS[1][:, 0:512], onesrowb_s[:], rowb[0:1, 512:1024],
                                start=True, stop=True)
                if halo:
                    eng.matmul(PS[2][:, 0:8], onesrowb_s[:], rowhb[0:1, 0:8],
                               start=True, stop=True)
                    i1 = eng.matmul(PS[3][:, 0:8], onesrowb_s[:], rowhb[0:1, 8:16],
                                    start=True, stop=True)
            X.inc(i1, "pe")
            X.wait("scalar", "pe")
            i1 = None
            if X.on("scalar"):
                eng.copy(rbc_s[:, 0:512], PS[0][:, 0:512])
                i1 = eng.copy(mbc_s[:, 0:512], PS[1][:, 0:512])
                if halo:
                    eng.copy(rbch_s[:, 0:8], PS[2][:, 0:8])
                    i1 = eng.copy(mbch_s[:, 0:8], PS[3][:, 0:8])
            X.inc(i1, "ac")
            for b_ in ([0, 1, 2, 3] if halo else [0, 1]):
                X.mark(("psfree", b_), "ac")
            bc_ac = X.c["ac"]
            for dt_ in range(D_TILES):
                bb = dt_ % 2
                X.wait_mark("gpsimd", ("xbffree", bb))
                X.dma("gpsimd", f"xa{bb}", xbf[bb][:, 0:xw],
                      xT[dt_ * 128:(dt_ + 1) * 128, xc0:xc0 + xw])
                X.wait_stream("vector", f"xa{bb}")
                X.wait("vector", "ac", bc_ac)
                if blk == 1:
                    X.wait_mark("vector", ("xlnfree",))
                i1 = None
                if X.on("vector"):
                    eng.tensor_mul(xln[dt_][:, main0:main0 + 512],
                                   xbf[bb][:, main0:main0 + 512], rbc_s[:, 0:512])
                    i1 = eng.tensor_sub(xln[dt_][:, main0:main0 + 512],
                                        xln[dt_][:, main0:main0 + 512], mbc_s[:, 0:512])
                    if halo:
                        i1
                        eng.tensor_mul(xln[dt_][:, 0:4], xbf[bb][:, 0:4], rbch_s[:, 0:4])
                        i1 = eng.tensor_sub(xln[dt_][:, 0:4], xln[dt_][:, 0:4],
                                            mbch_s[:, 0:4])
                X.inc(i1, "dv")
                X.mark(("xbffree", bb), "dv")
                X.mark(("xlnw", dt_), "dv")

        def emit_block(X, blk):
            eng = X.eng
            main0 = 4 if blk == 0 else 0
            xw = 516 if blk == 0 else 512
            xc0 = 0 if blk == 0 else 516
            halo = (blk == 0)

            emit_ln(X, blk, main0, xw, halo, xc0)

            if blk == 1:
                X.wait_mark("vector", ("hbcall",))
                i1 = None
                if X.on("vector"):
                    for j in range(40):
                        i1 = eng.tensor_copy(hbc[j][:, 0:4],
                                             tails[:, 4 * j:4 * j + 4])
                X.inc(i1, "dv")
                X.mark(("tailr",), "dv")

            for jo, ot in enumerate(range(32, 73)):
                sl = jo % 2
                pbank, pbank_id = (PS[5], 5) if jo % 2 == 0 else (PS[4], 4)
                hbank, hbank_id = (PS[3], 3) if jo % 2 == 0 else (PS[2], 2)
                j = ot - 32
                X.wait_mark("sync", ("wslab", sl))
                X.dma("sync", f"wsl{sl}", wslab[sl][:], winT[ot])

                X.wait_stream("tensor", f"wsl{sl}")
                X.wait_mark("tensor", ("psfree", pbank_id))
                if halo:
                    X.wait_mark("tensor", ("psfree", hbank_id))
                X.wait_mark("tensor", ("xlnw", D_TILES - 1))
                X.wait_mark("tensor", ("xlnw", 14))
                i1 = None
                if X.on("tensor"):
                    for dt_ in range(D_TILES):
                        i1 = eng.matmul(pbank[:, 0:512],
                                        wslab[sl][:, dt_ * 128:(dt_ + 1) * 128],
                                        xln[dt_][:, main0:main0 + 512],
                                        start=(dt_ == 0), stop=(dt_ == D_TILES - 1))
                    if halo:
                        for dt_ in range(D_TILES):
                            i1 = eng.matmul(hbank[:, 0:3],
                                            wslab[sl][:, dt_ * 128:(dt_ + 1) * 128],
                                            xln[dt_][:, 1:4],
                                            start=(dt_ == 0), stop=(dt_ == D_TILES - 1))
                X.inc(i1, "pe")
                X.mark(("wslab", sl), "pe")

                X.wait("scalar", "pe")
                if j < 40:
                    if blk == 1:
                        X.wait_mark("scalar", ("tailr",))
                        X.wait_mark("scalar", ("hbcall",))
                    i1 = None
                    if X.on("scalar"):
                        i1 = eng.copy(hbc[j][:, 4:516], pbank[:, 0:512])
                        if halo:
                            i1 = eng.copy(hbc[j][:, 1:4], hbank[:, 0:3])
                    X.inc(i1, "ac")
                    X.mark(("hbcr", j), "ac")
                else:
                    i1 = None
                    if X.on("scalar"):
                        i1 = eng.copy(dtr[0:64, main0:main0 + 512], pbank[0:64, 0:512])
                    X.inc(i1, "ac")
                    X.mark(("dtrw",), "ac")
                X.mark(("psfree", pbank_id), "ac")
                if halo:
                    X.mark(("psfree", hbank_id), "ac")

            for j in range(40):
                cb = j % 2
                X.wait_mark("vector", ("hbcr", j))
                X.wait_mark("vector", ("cvfree", cb))
                if blk == 1:
                    X.wait_mark("vector", ("tailr",))
                i1 = None
                if X.on("vector"):
                    eng.tensor_copy(tails[:, 4 * j:4 * j + 4], hbc[j][:, 512:516])
                    eng.tensor_scalar(cv1[cb][:], hbc[j][:, 1:513],
                                      convw_s[:, 4 * j:4 * j + 1], None, ALU.mult)
                    for tap in range(1, 4):
                        i1 = eng.scalar_tensor_tensor(
                            cv1[cb][:], hbc[j][:, 1 + tap:513 + tap],
                            convw_s[:, 4 * j + tap:4 * j + tap + 1],
                            cv1[cb][:], ALU.mult, ALU.add)
                X.inc(i1, "dv")
                X.mark(("tailw", j), "dv")
                X.wait("scalar", "dv")
                i1 = None
                if X.on("scalar"):
                    i1 = eng.activation(hbc[j][:, 4:516], cv1[cb][:], AF.Silu,
                                        bias=convb_s[:, j:j + 1])
                X.inc(i1, "ac")
                X.mark(("hs", j), "ac")
                X.mark(("cvfree", cb), "ac")

            X.wait_mark("vector", ("dtrw",))
            i1 = None
            if X.on("vector"):
                i1 = eng.tensor_scalar(dts1[:, 0:512], dtr[0:64, main0:main0 + 512],
                                       hvec_s[:, 0:1], None, ALU.add)
            X.inc(i1, "dv")
            X.wait("scalar", "dv")
            i1 = None
            if X.on("scalar"):
                eng.activation(dts2[:, 0:512], dts1[:, 0:512], AF.Abs)
                eng.activation(dts2[:, 0:512], dts2[:, 0:512], AF.Exp, scale=-1.0)
                i1 = eng.activation(dts2[:, 0:512], dts2[:, 0:512], AF.Ln,
                                    bias=epsv_s[0:64, 2:3])
            X.inc(i1, "ac")
            X.wait("vector", "ac")
            i1 = None
            if X.on("vector"):
                eng.tensor_scalar_max(dts1[:, 0:512], dts1[:, 0:512], 0.0)
                eng.tensor_add(dts1[:, 0:512], dts1[:, 0:512], dts2[:, 0:512])
                eng.tensor_scalar(As_s[:, 0:512], dts1[:, 0:512], expa_s[:],
                                  -1.0, ALU.mult, ALU.mult)
                eng.tensor_copy(dtb[:, 0:512], dts1[:, 0:512])
                for ccc in range(2):
                    i1 = eng.tensor_tensor_scan(acum[:, ccc * 256:(ccc + 1) * 256],
                                                As_s[:, ccc * 256:(ccc + 1) * 256],
                                                zer64[:], 0.0, ALU.add, ALU.bypass)
                eng.tensor_copy(ahi[:], acum[:, 0:512])
                eng.tensor_sub(arr_[:], acum[:, 0:512], ahi[:])
                eng.tensor_copy(alo[:], arr_[:])
                eng.tensor_sub(arr_[:], arr_[:], alo[:])
                i1 = eng.tensor_copy(alolo[:], arr_[:])
            X.inc(i1, "dv")
            X.mark(("acumw",), "dv")
            X.wait("scalar", "dv")
            i1 = None
            if X.on("scalar"):
                for ccc in range(2):
                    sl_ = slice(ccc * 256, (ccc + 1) * 256)
                    eng.activation(decay[:, sl_], acum[:, sl_], AF.Exp, scale=-1.0,
                                   bias=acum[:, ccc * 256 + 255:ccc * 256 + 256])
                    i1 = eng.activation(sdt[:, sl_], acum[:, sl_], AF.Exp)
            X.inc(i1, "ac")
            X.mark(("decayw",), "ac")

            for cc in range(2):
                emit_ssd_chunk(X, blk, cc)

            for ot in range(32):
                sl = ot % 2
                pbank, pbank_id = (PS[5], 5) if ot % 2 == 0 else (PS[4], 4)
                X.wait_mark("sync", ("wslab", sl))
                X.dma("sync", f"wsl{sl}", wslab[sl][:], winT[ot])
                X.wait_stream("tensor", f"wsl{sl}")
                X.wait_mark("tensor", ("psfree", pbank_id))
                i1 = None
                if X.on("tensor"):
                    for dt_ in range(D_TILES):
                        i1 = eng.matmul(pbank[:, 0:512],
                                        wslab[sl][:, dt_ * 128:(dt_ + 1) * 128],
                                        xln[dt_][:, main0:main0 + 512],
                                        start=(dt_ == 0), stop=(dt_ == D_TILES - 1))
                X.inc(i1, "pe")
                X.mark(("wslab", sl), "pe")
                gb = ot % 2
                X.wait("scalar", "pe")
                X.wait_mark("scalar", ("gsilfree", gb))
                i1 = None
                if X.on("scalar"):
                    i1 = eng.activation(gsil[gb][:], pbank[:, 0:512], AF.Silu)
                X.inc(i1, "ac")
                X.mark(("psfree", pbank_id), "ac")
                X.mark(("gsilw", gb), "ac")
                X.wait_mark("vector", ("gsilw", gb))
                X.wait_mark("vector", ("yready", ot))
                i1 = None
                if X.on("vector"):
                    i1 = eng.tensor_mul(y_sb[ot][:], y_sb[ot][:], gsil[gb][:])
                X.inc(i1, "dv")
                X.mark(("gsilfree", gb), "dv")
                X.mark(("yg", ot), "dv")
            X.mark(("xlnfree",), "pe")

            for ot in range(32):
                sq = ot % 2
                X.wait_mark("scalar", ("sqfree", sq))
                X.wait_mark("scalar", ("yg", ot))
                i1 = None
                if X.on("scalar"):
                    i1 = eng.activation(sqt[sq][:, 0:512], y_sb[ot][:], AF.Square)
                X.inc(i1, "ac")
                X.mark(("sqw", sq), "ac")
                X.wait_mark("tensor", ("sqw", sq))
                if ot == 0:
                    X.wait_mark("tensor", ("psfree", 0))
                i1 = None
                if X.on("tensor"):
                    i1 = eng.matmul(PS[0][0:1, 0:512], onescolb_s[:], sqt[sq][:, 0:512],
                                    start=(ot == 0), stop=(ot == 31))
                X.inc(i1, "pe")
                X.mark(("sqfree", sq), "pe")
            X.wait("vector", "pe")
            i1 = None
            if X.on("vector"):
                i1 = eng.tensor_scalar(rowf[0:1, 0:512], PS[0][0:1, 0:512],
                                       1.0 / INTER_A, None, ALU.mult)
            X.inc(i1, "dv")
            X.mark(("psfree", 0), "dv")
            X.wait("scalar", "dv")
            i1 = None
            if X.on("scalar"):
                eng.activation(rowf[0:1, 0:512], rowf[0:1, 0:512], AF.Ln,
                               bias=epsv_s[0:1, 1:2])
                i1 = eng.activation(rowf[0:1, 0:512], rowf[0:1, 0:512], AF.Exp, scale=-0.5)
            X.inc(i1, "ac")
            X.wait("vector", "ac")
            i1 = None
            if X.on("vector"):
                i1 = eng.tensor_copy(rowb[0:1, 0:512], rowf[0:1, 0:512])
            X.inc(i1, "dv")
            X.wait("tensor", "dv")
            X.wait_mark("tensor", ("psfree", 1))
            i1 = None
            if X.on("tensor"):
                i1 = eng.matmul(PS[1][:, 0:512], onesrowb_s[:], rowb[0:1, 0:512],
                                start=True, stop=True)
            X.inc(i1, "pe")
            X.wait("scalar", "pe")
            i1 = None
            if X.on("scalar"):
                i1 = eng.copy(rbc_s[:, 0:512], PS[1][:, 0:512])
            X.inc(i1, "ac")
            X.mark(("psfree", 1), "ac")
            rms_ac = X.c["ac"]
            for ot in range(32):
                X.wait("vector", "ac", rms_ac)
                i1 = None
                if X.on("vector"):
                    i1 = eng.scalar_tensor_tensor(y_sb[ot][:], y_sb[ot][:],
                                                  normw_s[:, ot:ot + 1], rbc_s[:, 0:512],
                                                  ALU.mult, ALU.mult)
                X.inc(i1, "dv")
                X.mark(("yn", ot), "dv")

            for et in range(16):
                for half in range(2):
                    sl = (2 * et + half) % 2
                    X.wait_mark("sync", ("wslab", sl))
                    X.dma("sync", f"wsl{sl}", woslab[sl][:],
                          woutT[et, :, half * 2048:(half + 1) * 2048])
                pbank, pbank_id = (PS[5], 5) if et % 2 == 0 else (PS[4], 4)
                X.wait_mark("tensor", ("psfree", pbank_id))
                X.wait_mark("tensor", ("yn", IA_TILES - 1))
                for half in range(2):
                    sl = (2 * et + half) % 2
                    X.wait_stream("tensor", f"wsl{sl}")
                    i1 = None
                    if X.on("tensor"):
                        for kt in range(16):
                            kg = 16 * half + kt
                            i1 = eng.matmul(pbank[:, 0:512],
                                            woslab[sl][:, kt * 128:(kt + 1) * 128],
                                            y_sb[kg][:],
                                            start=(kg == 0), stop=(kg == IA_TILES - 1))
                    X.inc(i1, "pe")
                    X.mark(("wslab", sl), "pe")
                xb_ = et % 2
                X.wait_mark("sync", ("xresfree", xb_))
                X.wait_stream("sync", f"x1s{xb_}")
                X.dma("sync", f"xr{xb_}", xres[xb_][:],
                      xT[et * 128:(et + 1) * 128, 4 + blk * 512:4 + (blk + 1) * 512])
                X.wait("vector", "pe")
                X.wait_stream("vector", f"xr{xb_}")
                i1 = None
                if X.on("vector"):
                    i1 = eng.scalar_tensor_tensor(xres[xb_][:], pbank[:, 0:512], 1.0,
                                                  xres[xb_][:], ALU.mult, ALU.add)
                X.inc(i1, "dv")
                X.mark(("psfree", pbank_id), "dv")
                X.mark(("x1", xb_), "dv")
                X.wait_mark("sync", ("x1", xb_))
                X.dma("sync", f"x1s{xb_}", x1T[et * 128:(et + 1) * 128,
                                               blk * 512:(blk + 1) * 512],
                      xres[xb_][:])
                X.wait_mark("scalar", ("x1", xb_))
                X.wait_stream("scalar", "x1bs")
                i1 = None
                if X.on("scalar"):
                    i1 = eng.activation(x1bb[0][:], xres[xb_][:], AF.Copy,
                                        bias=0.0)
                X.inc(i1, "ac")
                X.mark(("x1b",), "ac")
                X.mark(("xresfree", xb_), "ac")
                X.wait_mark("gpsimd", ("x1b",))
                X.dma("gpsimd", "x1bs",
                      x1bT[et * 128:(et + 1) * 128, blk * 512:(blk + 1) * 512],
                      x1bb[0][:])
            X.mark(("yfree",), "pe")
            if blk == 0:
                X.mark(("hbcall",), "dv")

        def emit_ssd_chunk(X, blk, cc):
            eng = X.eng
            hcsl = slice(4 + cc * 256, 4 + (cc + 1) * 256)

            for st in range(2):
                X.wait_mark("tensor", ("acumw",))
                X.wait_mark("tensor", ("psfree", 0))
                i1 = None
                if X.on("tensor"):
                    i1 = eng.transpose(PS[0][:, 0:64],
                                       acum[:, cc * 256 + st * 128:cc * 256 + (st + 1) * 128],
                                       identf_s[0:64, 0:64])
                X.inc(i1, "pe")
                X.wait("scalar", "pe")
                X.wait_mark("scalar", ("atokfree", st))
                i1 = None
                if X.on("scalar"):
                    i1 = eng.copy(atok[st][:], PS[0][:, 0:64])
                X.inc(i1, "ac")
                X.mark(("psfree", 0), "ac")
                X.mark(("atokw", st), "ac")

            for g in range(8):
                bt = hbc[32 + g // 2]
                ct = hbc[36 + g // 2]
                rbase = (g % 2) * 64
                for st in range(2):
                    gb_, gb_id = (PS[1], 1) if (2 * g + st) % 2 == 0 else (PS[3], 3)
                    X.wait_mark("tensor", ("psfree", gb_id))
                    X.wait_mark("tensor", ("hs", 32 + g // 2))
                    X.wait_mark("tensor", ("hs", 36 + g // 2))
                    i1 = None
                    if X.on("tensor"):
                        i1 = eng.matmul(gb_[:, 0:256],
                                        bt[rbase:rbase + 64,
                                           4 + cc * 256 + st * 128:4 + cc * 256 + (st + 1) * 128],
                                        ct[rbase:rbase + 64, hcsl],
                                        start=True, stop=True)
                    X.inc(i1, "pe")
                    X.wait("vector", "pe")
                    X.wait_mark("vector", ("gmfree", st, g))
                    i1 = None
                    if X.on("vector"):
                        i1 = eng.scalar_tensor_tensor(
                            gmall2[:, g * 512 + st * 256:g * 512 + (st + 1) * 256],
                            gb_[:, 0:256], 1.0,
                            tri_s[st][:], ALU.mult, ALU.mult)
                    X.inc(i1, "dv")
                    X.mark(("psfree", gb_id), "dv")
                    X.mark(("gmw", st, g), "dv")

            for p_ in (0, 1):
                if ("psfree", 2) in X.marks:
                    X.marks[("psfree2", p_)] = X.marks[("psfree", 2)]
                if ("psfree", 4) in X.marks:
                    X.marks[("psfree4", p_)] = X.marks[("psfree", 4)]
            for jj in range(33):
                if jj < 32:
                    emit_ssd_pair_fh(X, blk, cc, jj)
                if jj >= 1:
                    emit_ssd_pair_sh(X, blk, cc, jj - 1)

            X.mark(("psfree", 2), "dv")
            X.mark(("psfree", 4), "dv")
            for st in range(2):
                X.mark(("atokfree", st), "dv")
                for g in range(8):
                    X.mark(("gmfree", st, g), "gp")

        def emit_ssd_pair_fh(X, blk, cc, j):
            eng = X.eng
            csl = slice(cc * 256, (cc + 1) * 256)
            hcsl = slice(4 + cc * 256, 4 + (cc + 1) * 256)
            a = j % 4
            p = j % 2
            bA, bA_id = (PS[0], 0) if p == 0 else (PS[1], 1)
            bD, bD_id = (PS[3], 3) if p == 0 else (PS[5], 5)

            X.wait_mark("tensor", ("psfree", bD_id))
            X.wait_mark("tensor", ("hs", j))
            i1 = None
            if X.on("tensor"):
                i1 = eng.matmul(bD[:, 0:256], psel_s[:, j * 128:(j + 1) * 128],
                                dtb[:, csl], start=True, stop=True)
            X.inc(i1, "pe")
            X.wait("vector", "pe")
            X.wait_mark("vector", ("hsdtfree", p))
            X.wait_mark("vector", ("hsdtpe", p))
            i1 = None
            if X.on("vector"):
                i1 = eng.tensor_mul(hsdt1[p][:], hbc[j][:, hcsl], bD[:, 0:256])
            X.inc(i1, "dv")
            X.mark(("psfree", bD_id), "dv")
            X.mark(("hsdtw", p), "dv")

            for st in range(2):
                X.wait_mark("tensor", ("hsdtw", p))
                X.wait_mark("tensor", ("ptfree", st))
                i1 = None
                if X.on("tensor"):
                    i1 = eng.transpose(PT[st][:, 0:128],
                                       hsdt1[p][:, st * 128:(st + 1) * 128],
                                       identb_s[:])
                X.inc(i1, "pe")
                X.wait("scalar", "pe")
                X.wait_mark("scalar", ("htokfree", p, st))
                i1 = None
                if X.on("scalar"):
                    i1 = eng.copy(htokp[p][st][:], PT[st][:, 0:128])
                X.inc(i1, "ac")
                X.mark(("ptfree", st), "ac")
                X.mark(("htokw", p, st), "ac")
            X.mark(("hsdtpe", p), "pe")

            h = 2 * j
            g = h % 8
            X.wait_mark("tensor", ("psfree", bA_id))
            X.wait_mark("tensor", ("acumw",))
            i1 = None
            if X.on("tensor"):
                eng.matmul(bA[:, 0:256], ssel_s[:, h * 128:(h + 1) * 128],
                           ahi[:, csl], start=True, stop=False)
                eng.matmul(bA[:, 0:256], ssel_s[:, h * 128:(h + 1) * 128],
                           alo[:, csl], start=False, stop=False)
                i1 = eng.matmul(bA[:, 0:256], ssel_s[:, h * 128:(h + 1) * 128],
                                alolo[:, csl], start=False, stop=True)
            X.inc(i1, "pe")
            bca_pe = X.c["pe"]
            X.wait("vector", "pe", bca_pe)
            X.wait_mark("vector", ("dscfree", p))
            for st in range(2):
                X.wait_mark("vector", ("atokw", st))
                i1 = None
                if X.on("vector"):
                    i1 = eng.tensor_scalar(dsc[p][:, st * 256:(st + 1) * 256],
                                           bA[:, 0:256],
                                           atok[st][:, h:h + 1], 0.0,
                                           ALU.subtract, ALU.min)
                X.inc(i1, "dv")
            X.mark(("dscw", p), "dv")
            X.mark(("psfree", bA_id), "dv")
            X.wait_mark("scalar", ("dscw", p))
            X.wait_mark("scalar", ("escfree", p))
            i1 = None
            if X.on("scalar"):
                i1 = eng.activation(esc[p][:], dsc[p][:], AF.Exp)
            X.inc(i1, "ac")
            X.mark(("dscfree", p), "ac")
            X.mark(("escw", p), "ac")
            X.wait_mark("gpsimd", ("escw", p))
            X.wait_mark("gpsimd", ("gmw", 0, g))
            X.wait_mark("gpsimd", ("gmw", 1, g))
            X.wait_mark("gpsimd", ("mscfree", p))
            i1 = None
            if X.on("gpsimd"):
                i1 = eng.tensor_mul(msc[p][:], esc[p][:],
                                    gmall2[:, g * 512:(g + 1) * 512])
            X.inc(i1, "gp")
            X.mark(("escfree", p), "gp")
            X.mark(("mscw", p), "gp")

            X.wait_mark("gpsimd", ("ppfree", p))
            X.wait_mark("gpsimd", ("hsdtw", p))
            X.wait_mark("gpsimd", ("hs", 32 + a))
            i1 = None
            if X.on("gpsimd"):
                i1 = eng.tensor_mul(pp_s[p][:], hbc[32 + a][:, hcsl], hsdt1[p][:])
            X.inc(i1, "gp")
            X.mark(("ppw", p), "gp")
            X.mark(("hsdtfree", p), "gp")

        def emit_ssd_pair_sh(X, blk, cc, j):
            eng = X.eng
            csl = slice(cc * 256, (cc + 1) * 256)
            hcsl = slice(4 + cc * 256, 4 + (cc + 1) * 256)
            a = j % 4
            p = j % 2
            co = p * 256
            bA, bA_id = (PS[0], 0) if p == 0 else (PS[1], 1)
            bD, bD_id = (PS[3], 3) if p == 0 else (PS[5], 5)
            bc_tile_c = hbc[36 + a]

            X.wait_mark("tensor", ("psfree2", p))
            X.wait_mark("tensor", ("mscw", p))
            for st in range(2):
                X.wait_mark("tensor", ("htokw", p, st))
            i1 = None
            if X.on("tensor"):
                for st in range(2):
                    i1 = eng.matmul(PS[2][0:64, co:co + 256],
                                    htokp[p][st][:, 0:64],
                                    msc[p][:, st * 256:(st + 1) * 256],
                                    start=(st == 0), stop=(st == 1))
            X.inc(i1, "pe")
            X.mark(("mscfree", p), "pe")

            X.wait_mark("tensor", ("psfree", bD_id))
            X.wait_mark("tensor", ("decayw",))
            i1 = None
            if X.on("tensor"):
                i1 = eng.matmul(bD[:, 0:256], psel_s[:, j * 128:(j + 1) * 128],
                                decay[:, csl], start=True, stop=True)
            X.inc(i1, "pe")
            X.wait_mark("tensor", ("psfree4", p))
            i1 = None
            if X.on("tensor"):
                i1 = eng.matmul(PS[4][:, co:co + 256], psel_s[:, j * 128:(j + 1) * 128],
                                sdt[:, csl], start=True, stop=True)
            X.inc(i1, "pe")
            bc_pe = X.c["pe"]
            X.wait("vector", "pe", bc_pe)
            X.wait_mark("vector", ("ppw", p))
            X.wait_mark("vector", ("stcolfree", p))
            i1 = None
            if X.on("vector"):
                eng.tensor_mul(pp2_s[p][:], pp_s[p][:], bD[:, 0:256])
                i1 = eng.tensor_reduce(stcol[p][:], pp2_s[p][:], AX.X, ALU.add)
            X.inc(i1, "dv")
            X.mark(("ppfree", p), "dv")
            X.mark(("psfree", bD_id), "dv")
            X.mark(("stcolw", p), "dv")

            h = 2 * j + 1
            g = h % 8
            X.wait_mark("tensor", ("psfree", bA_id))
            X.wait_mark("tensor", ("acumw",))
            i1 = None
            if X.on("tensor"):
                eng.matmul(bA[:, 0:256], ssel_s[:, h * 128:(h + 1) * 128],
                           ahi[:, csl], start=True, stop=False)
                eng.matmul(bA[:, 0:256], ssel_s[:, h * 128:(h + 1) * 128],
                           alo[:, csl], start=False, stop=False)
                i1 = eng.matmul(bA[:, 0:256], ssel_s[:, h * 128:(h + 1) * 128],
                                alolo[:, csl], start=False, stop=True)
            X.inc(i1, "pe")
            bca_pe = X.c["pe"]
            X.wait("vector", "pe", bca_pe)
            X.wait_mark("vector", ("dscfree", p))
            for st in range(2):
                X.wait_mark("vector", ("atokw", st))
                i1 = None
                if X.on("vector"):
                    i1 = eng.tensor_scalar(dsc[p][:, st * 256:(st + 1) * 256],
                                           bA[:, 0:256],
                                           atok[st][:, h:h + 1], 0.0,
                                           ALU.subtract, ALU.min)
                X.inc(i1, "dv")
            X.mark(("dscw", p), "dv")
            X.mark(("psfree", bA_id), "dv")
            X.wait_mark("scalar", ("dscw", p))
            X.wait_mark("scalar", ("escfree", p))
            i1 = None
            if X.on("scalar"):
                i1 = eng.activation(esc[p][:], dsc[p][:], AF.Exp)
            X.inc(i1, "ac")
            X.mark(("dscfree", p), "ac")
            X.mark(("escw", p), "ac")
            X.wait_mark("gpsimd", ("escw", p))
            X.wait_mark("gpsimd", ("gmw", 0, g))
            X.wait_mark("gpsimd", ("gmw", 1, g))
            X.wait_mark("gpsimd", ("mscfree", p))
            i1 = None
            if X.on("gpsimd"):
                i1 = eng.tensor_mul(msc[p][:], esc[p][:],
                                    gmall2[:, g * 512:(g + 1) * 512])
            X.inc(i1, "gp")
            X.mark(("escfree", p), "gp")
            X.mark(("mscw", p), "gp")

            X.wait_mark("tensor", ("mscw", p))
            i1 = None
            if X.on("tensor"):
                for st in range(2):
                    i1 = eng.matmul(PS[2][64:128, co:co + 256],
                                    htokp[p][st][:, 64:128],
                                    msc[p][:, st * 256:(st + 1) * 256],
                                    start=(st == 0), stop=(st == 1))
            X.inc(i1, "pe")
            X.mark(("mscfree", p), "pe")
            X.mark(("htokfree", p, 0), "pe")
            X.mark(("htokfree", p, 1), "pe")

            yd_pe = X.c["pe"]
            X.wait("vector", "pe", yd_pe)
            X.wait_mark("vector", ("stcolw", p))
            X.wait_mark("vector", ("yofree", p))
            i1 = None
            if X.on("vector"):
                i1 = eng.scalar_tensor_tensor(yo_s[p][:], bc_tile_c[:, hcsl],
                                              stcol[p][:],
                                              PS[4][:, co:co + 256], ALU.mult, ALU.mult)
            X.inc(i1, "dv")
            X.mark(("psfree4", p), "dv")
            X.mark(("stcolfree", p), "dv")
            i1 = None
            if X.on("vector"):
                i1 = eng.scalar_tensor_tensor(yt_s[p][:], hbc[j][:, hcsl],
                                              dpair_s[:, j:j + 1], yo_s[p][:],
                                              ALU.mult, ALU.add)
            X.inc(i1, "dv")
            X.mark(("yofree", p), "dv")
            if cc == 0:
                X.wait_mark("vector", ("yfree",))
            i1 = None
            if X.on("vector"):
                i1 = eng.tensor_add(y_sb[j][:, csl], yt_s[p][:], PS[2][:, co:co + 256])
            X.inc(i1, "dv")
            X.mark(("psfree2", p), "dv")
            if cc == 1:
                X.mark(("yready", j), "dv")

        @block.sync
        def _(eng):
            emit(Ctx("sync", eng, sems, spool))

        @block.vector
        def _(eng):
            emit(Ctx("vector", SerialEng(eng, auto["dv"]), sems, spool))

        @block.scalar
        def _(eng):
            emit(Ctx("scalar", SerialEng(eng, auto["ac"]), sems, spool))

        @block.tensor
        def _(eng):
            emit(Ctx("tensor", eng, sems, spool))

        @block.gpsimd
        def _(eng):
            X = emit(Ctx("gpsimd", SerialEng(eng, auto["gp"]), sems, spool))
            for name in X.streams:
                idx, cnt = X.streams[name]
                if cnt:
                    eng.wait_ge(spool[idx], cnt)

    return nc



def prep_launch_a_inputs(inputs):
    _ = inputs.get("L0_attn_a_mask")
    x = np.asarray(inputs["L0_attn_a_input_states"], np.float32)
    W_in = np.asarray(inputs["L0_attn_a_W_in"], np.float32)
    conv_w = np.asarray(inputs["L0_attn_a_conv_w"], np.float32)[:, 0, :]
    conv_b = np.asarray(inputs["L0_attn_a_conv_b"], np.float32)
    dt_bias = np.asarray(inputs["L0_attn_a_dt_bias"], np.float32)
    A_log = np.asarray(inputs["L0_attn_a_A_log"], np.float32)
    Dv = np.asarray(inputs["L0_attn_a_D"], np.float32)
    norm_w = np.asarray(inputs["L0_attn_a_norm_w"], np.float32)
    W_out = np.asarray(inputs["L0_attn_a_W_out"], np.float32)
    lut = np.asarray(inputs["L0_ffn_lut"], np.float32)
    gate_idx = np.asarray(inputs["L0_ffn_gate_idx"], np.int32)
    up_idx = np.asarray(inputs["L0_ffn_up_idx"], np.int32)
    down_idx = np.asarray(inputs["L0_ffn_down_idx"], np.int32)

    shared = {
        "winT": tile_lhsT(W_in, D_TILES).astype(BF),
        "woutT": tile_lhsT(W_out, IA_TILES).astype(BF),
        "convw": np.ascontiguousarray(conv_w.reshape(40, 128, 4).transpose(1, 0, 2).reshape(128, 160)),
        "convb": np.ascontiguousarray(conv_b.reshape(40, 128).T),
        "normw": np.ascontiguousarray(norm_w.reshape(32, 128).T),
        "hvec": np.ascontiguousarray(np.stack([dt_bias, A_log], axis=1)),
        "dpair": np.ascontiguousarray(
            np.where(np.arange(128)[:, None] < 64, Dv[2 * np.arange(32)][None, :],
                     Dv[2 * np.arange(32) + 1][None, :]).astype(np.float32)),
        "lutbc": np.ascontiguousarray(np.tile(lut[None, :], (128, 1))),
        "pselb": selpack_pair(),
        "sselb": selpack_single(),
        "tri0": trimask_np(0),
        "tri1": trimask_np(1),
        "identb": np.eye(128, dtype=BF),
        "identf": np.eye(128, dtype=np.float32),
        "onescolb": np.ones((128, 1), BF),
        "onesrowb": np.ones((1, 128), BF),
        "epsv": np.tile(np.array([[LN_EPS, RMS_EPS, 1.0, 0.0, 0.0, 0.0, 0.0, 0.0]], np.float32), (128, 1)),
    }

    gT = np.ascontiguousarray(gate_idx.T.astype(BF))
    uT = np.ascontiguousarray(up_idx.T.astype(BF))
    dT = np.ascontiguousarray(down_idx.T.astype(BF))

    per_core = []
    for c in range(NCORES):
        b, q = divmod(c, 4)
        t0 = q * TPC
        xc = np.zeros((E, 1028), np.float32)
        lo = max(0, t0 - 4)
        pad = 4 - (t0 - lo)
        xc[:, pad:] = x[b, lo:t0 + TPC, :].T
        m = dict(shared)
        m["xT"] = np.ascontiguousarray(xc)
        m["gidx"] = np.ascontiguousarray(gT[256 * c:256 * (c + 1), :])
        m["uidx"] = np.ascontiguousarray(uT[256 * c:256 * (c + 1), :])
        m["didx"] = np.ascontiguousarray(dT[688 * c:688 * (c + 1), :])
        per_core.append(m)
    return per_core


def kernel(**inputs):
    raise NotImplementedError("launches B/C not wired yet")


if __name__ == "__main__":
    nc = build_launch_a()
    print("built launch A:", nc.n_instructions(), "instructions")
import math as _math

IF_TILES = INTER_F // 128


def _lam():
    start = 2.0 ** (-2.0 ** (-(_math.log2(16) - 3)))
    slopes = np.array([start ** (i + 1) for i in range(16)], dtype=np.float64)
    return np.exp(-slopes)


def build_launch_b():
    nc = bass.Bass("TRN2", target_bir_lowering=False)

    x1T = nc.dram_tensor("x1T", [E, TPC], F32, kind="ExternalInput").ap()
    x1bT = nc.dram_tensor("x1bT", [E, TPC], BF16, kind="ExternalInput").ap()
    wgT = nc.dram_tensor("wgTt", [IF_TILES, 128, E], BF16, kind="ExternalInput").ap()
    wuT = nc.dram_tensor("wuTt", [IF_TILES, 128, E], BF16, kind="ExternalInput").ap()
    wdT = nc.dram_tensor("wdTt", [D_TILES, 128, INTER_F], BF16, kind="ExternalInput").ap()
    wqkvT = nc.dram_tensor("wqkvT", [48, 128, D_TILES * 128], BF16, kind="ExternalInput").ap()
    bqkv = nc.dram_tensor("bqkv", [128, 48], F32, kind="ExternalInput").ap()
    dkcol = nc.dram_tensor("dkcol", [128, 16], F32, kind="ExternalInput").ap()
    epsvd = nc.dram_tensor("epsv", [128, 8], F32, kind="ExternalInput").ap()
    onescolbd = nc.dram_tensor("onescolb", [128, 1], BF16, kind="ExternalInput").ap()
    onesrowbd = nc.dram_tensor("onesrowb", [1, 128], BF16, kind="ExternalInput").ap()
    identbd = nc.dram_tensor("identb", [128, 128], BF16, kind="ExternalInput").ap()

    x2T = nc.dram_tensor("x2T", [E, TPC], F32, kind="ExternalOutput").ap()
    qT = nc.dram_tensor("qT", [E, TPC], BF16, kind="ExternalOutput").ap()
    kfT = nc.dram_tensor("kfT", [E, TPC], BF16, kind="ExternalOutput").ap()
    ktokT = nc.dram_tensor("ktokT", [TPC, E], BF16, kind="ExternalOutput").ap()
    vtokT = nc.dram_tensor("vtokT", [TPC, E], BF16, kind="ExternalOutput").ap()
    TJ = nc.dram_tensor("TJ", [E, 128], F32, kind="ExternalOutput").ap()

    lam = _lam()
    lam_c = [float(v) for v in lam ** 128]
    ROT = ("sync", "scalar", "gpsimd")

    from contextlib import ExitStack
    with ExitStack() as ctx:
        en = ctx.enter_context
        sb = lambda name, shape, dt: en(nc.sbuf_tensor(name, shape, dt))

        epsv_s = sb("epsv_s", [128, 8], F32)
        onescolb_s = sb("onescolb_s", [128, 1], BF16)
        onesrowb_s = sb("onesrowb_s", [1, 128], BF16)
        identb_s = sb("identb_s", [128, 128], BF16)
        bqkv_s = sb("bqkv_s", [128, 48], F32)
        dkcol_s = sb("dkcol_s", [128, 16], F32)

        xln = [sb(f"xln{i}", [128, TPC], BF16) for i in range(D_TILES)]
        sqt = [sb(f"sqt{i}", [128, TPC], BF16) for i in range(2)]
        xres = [sb(f"xres{i}", [128, 512], F32) for i in range(4)]
        x2t = [sb(f"x2t{i}", [128, TPC], F32) for i in range(2)]
        rowf = sb("rowf", [1, 3072], F32)
        rowb = sb("rowb", [1, 2048], BF16)
        rbc_s = sb("rbc_s", [128, TPC], BF16)
        mbc_s = sb("mbc_s", [128, TPC], BF16)
        h_sb = [sb(f"h{i}", [128, TPC], BF16) for i in range(IF_TILES)]
        gs_s = [sb(f"gs{i}", [128, 512], BF16) for i in range(2)]
        wslab = [sb(f"wslab{i}", [128, 5504], BF16) for i in range(2)]
        w2slab = [sb(f"w2slab{i}", [128, 2048], BF16) for i in range(2)]
        qkvt = [sb(f"qkvt{i}", [128, TPC], BF16) for i in range(2)]
        ktok_sb = [sb(f"ktok{m}", [128, 128], BF16) for m in range(8)]
        vt_s = [sb(f"vt{i}", [128, 128], BF16) for i in range(2)]
        kd_s = sb("kd_s", [128, 128], BF16)
        srun = sb("srun", [128, 16 * 128], F32)

        PS = [en(nc.psum_tensor(f"PS{i}", [128, 512], F32)) for i in range(6)]
        PT = [en(nc.psum_tensor(f"PT{i}", [128, 1024], BF16)) for i in range(2)]

        sems = {n: en(nc.semaphore("s_" + n)) for n in SEMNAMES}
        spool = [en(nc.semaphore(f"sp{i}")) for i in range(52)]
        auto = {n: en(nc.semaphore("a_" + n)) for n in ["dv", "ac", "gp"]}
        block = en(nc.Block())


        def emit_ln_w(X, src):
            eng = X.eng
            for dt_ in range(D_TILES):
                bb = dt_ % 2
                if src is not None:
                    qe = ("sync", "scalar")[dt_ % 2]
                    X.wait_mark(qe, ("xlnfree", dt_))
                    X.dma(qe, f"xa{dt_}", xln[dt_][:],
                          src[dt_ * 128:(dt_ + 1) * 128, :])
                    X.wait_stream("scalar", f"xa{dt_}")
                else:
                    X.wait_mark("scalar", ("x2b", dt_))
                X.wait_mark("scalar", ("sqfree", bb))
                i1 = None
                if X.on("scalar"):
                    i1 = eng.activation(sqt[bb][:], xln[dt_][:], AF.Square)
                X.inc(i1, "ac")
                X.mark(("sqw", bb), "ac")
                X.wait_mark("tensor", ("sqw", bb))
                if src is not None:
                    X.wait_stream("tensor", f"xa{dt_}")
                if dt_ == 0:
                    for b_ in range(4):
                        X.wait_mark("tensor", ("psfree", b_))
                i1 = None
                if X.on("tensor"):
                    eng.matmul(PS[0][0:1, 0:512], onescolb_s[:], xln[dt_][:, 0:512],
                               start=(dt_ == 0), stop=(dt_ == D_TILES - 1))
                    eng.matmul(PS[1][0:1, 0:512], onescolb_s[:], xln[dt_][:, 512:1024],
                               start=(dt_ == 0), stop=(dt_ == D_TILES - 1))
                    eng.matmul(PS[2][0:1, 0:512], onescolb_s[:], sqt[bb][:, 0:512],
                               start=(dt_ == 0), stop=(dt_ == D_TILES - 1))
                    i1 = eng.matmul(PS[3][0:1, 0:512], onescolb_s[:],
                                    sqt[bb][:, 512:1024],
                                    start=(dt_ == 0), stop=(dt_ == D_TILES - 1))
                X.inc(i1, "pe")
                X.mark(("sqfree", bb), "pe")
                X.mark(("xstat", dt_), "pe")
            X.wait("vector", "pe")
            i1 = None
            if X.on("vector"):
                eng.tensor_scalar(rowf[0:1, 0:512], PS[0][0:1, 0:512], 1.0 / E,
                                  None, ALU.mult)
                eng.tensor_scalar(rowf[0:1, 512:1024], PS[1][0:1, 0:512], 1.0 / E,
                                  None, ALU.mult)
                eng.tensor_scalar(rowf[0:1, 1024:1536], PS[2][0:1, 0:512], 1.0 / E,
                                  None, ALU.mult)
                eng.tensor_scalar(rowf[0:1, 1536:2048], PS[3][0:1, 0:512], 1.0 / E,
                                  None, ALU.mult)
                eng.tensor_mul(rowf[0:1, 2048:3072], rowf[0:1, 0:1024],
                               rowf[0:1, 0:1024])
                i1 = eng.tensor_sub(rowf[0:1, 2048:3072], rowf[0:1, 1024:2048],
                                    rowf[0:1, 2048:3072])
            X.inc(i1, "dv")
            for b_ in range(4):
                X.mark(("psfree", b_), "dv")
            X.wait("scalar", "dv")
            i1 = None
            if X.on("scalar"):
                eng.activation(rowf[0:1, 2048:3072], rowf[0:1, 2048:3072], AF.Ln,
                               bias=epsv_s[0:1, 0:1])
                i1 = eng.activation(rowf[0:1, 2048:3072], rowf[0:1, 2048:3072],
                                    AF.Exp, scale=-0.5)
            X.inc(i1, "ac")
            X.wait("vector", "ac")
            i1 = None
            if X.on("vector"):
                eng.tensor_copy(rowb[0:1, 0:1024], rowf[0:1, 2048:3072])
                eng.tensor_mul(rowf[0:1, 0:1024], rowf[0:1, 0:1024],
                               rowf[0:1, 2048:3072])
                i1 = eng.tensor_copy(rowb[0:1, 1024:2048], rowf[0:1, 0:1024])
            X.inc(i1, "dv")
            X.wait("tensor", "dv")
            for b_ in range(4):
                X.wait_mark("tensor", ("psfree", b_))
            i1 = None
            if X.on("tensor"):
                eng.matmul(PS[0][:, 0:512], onesrowb_s[:], rowb[0:1, 0:512],
                           start=True, stop=True)
                eng.matmul(PS[1][:, 0:512], onesrowb_s[:], rowb[0:1, 512:1024],
                           start=True, stop=True)
                eng.matmul(PS[2][:, 0:512], onesrowb_s[:], rowb[0:1, 1024:1536],
                           start=True, stop=True)
                i1 = eng.matmul(PS[3][:, 0:512], onesrowb_s[:], rowb[0:1, 1536:2048],
                                start=True, stop=True)
            X.inc(i1, "pe")
            X.wait("scalar", "pe")
            i1 = None
            if X.on("scalar"):
                eng.copy(rbc_s[:, 0:512], PS[0][:, 0:512])
                eng.copy(rbc_s[:, 512:1024], PS[1][:, 0:512])
                eng.copy(mbc_s[:, 0:512], PS[2][:, 0:512])
                i1 = eng.copy(mbc_s[:, 512:1024], PS[3][:, 0:512])
            X.inc(i1, "ac")
            for b_ in range(4):
                X.mark(("psfree", b_), "ac")
            bc_ac = X.c["ac"]
            for dt_ in range(D_TILES):
                ae = "vector" if dt_ % 3 != 2 else "gpsimd"
                asem = "dv" if ae == "vector" else "gp"
                X.wait(ae, "ac", bc_ac)
                X.wait_mark(ae, ("xstat", dt_))
                i1 = None
                if X.on(ae):
                    eng.tensor_mul(xln[dt_][:], xln[dt_][:], rbc_s[:])
                    i1 = eng.tensor_sub(xln[dt_][:], xln[dt_][:], mbc_s[:])
                X.inc(i1, asem)
                X.mark(("xlnw", dt_), asem)

        def emit(X: Ctx):
            eng = X.eng
            wcnt = [0]
            for dst, dsrc in [(epsv_s[:], epsvd), (onescolb_s[:], onescolbd),
                              (onesrowb_s[:], onesrowbd), (identb_s[:], identbd),
                              (bqkv_s[:], bqkv), (dkcol_s[:], dkcol)]:
                X.dma("sync", "c", dst, dsrc)
            for e_ in ["vector", "scalar", "tensor", "gpsimd"]:
                X.wait_stream(e_, "c")
            i1 = None
            if X.on("vector"):
                i1 = eng.memset(srun[:], 0.0)
            X.inc(i1, "dv")
            X.mark(("srunf",), "dv")

            emit_ln_w(X, x1bT)

            for j in range(IF_TILES):
                sl = wcnt[0] % 2
                qi = wcnt[0] % 3
                qe = ROT[qi]
                wcnt[0] += 1
                qi2 = (qi + 1) % 3
                qe2 = ROT[qi2]
                X.wait_mark(qe, ("wslab", sl))
                X.dma(qe, f"wsl{qi}{sl}", wslab[sl][:, 0:2048], wgT[j])
                X.wait_mark(qe2, ("wslab", sl))
                X.dma(qe2, f"wsu{qi2}{sl}", wslab[sl][:, 2048:4096], wuT[j])
                X.wait_stream("tensor", f"wsl{qi}{sl}")
                X.wait_stream("tensor", f"wsu{qi2}{sl}")
                X.wait_mark("tensor", ("xlnw", D_TILES - 1))
                X.wait_mark("tensor", ("xlnw", 14))
                for (grp, bank, bid, wofs, h0) in [
                        ("pg0", PS[5], 5, 0, 0), ("pu0", PS[3], 3, 2048, 0),
                        ("pg1", PS[4], 4, 0, 512), ("pu1", PS[2], 2, 2048, 512)]:
                    X.wait_mark("tensor", ("psfree", bid))
                    i1 = None
                    if X.on("tensor"):
                        for dt_ in range(D_TILES):
                            i1 = eng.matmul(
                                bank[:, 0:512],
                                wslab[sl][:, wofs + dt_ * 128:wofs + (dt_ + 1) * 128],
                                xln[dt_][:, h0:h0 + 512],
                                start=(dt_ == 0), stop=(dt_ == D_TILES - 1))
                    X.inc(i1, "pe")
                    X.mark((grp,), "pe")
                X.mark(("wslab", sl), "pe")
                for half, pg_, pu_, pgb, pub in [(0, PS[5], PS[3], 5, 3),
                                                 (1, PS[4], PS[2], 4, 2)]:
                    X.wait_mark("scalar", (f"pg{half}",))
                    X.wait_mark("scalar", ("gsfree", half))
                    i1 = None
                    if X.on("scalar"):
                        i1 = eng.activation(gs_s[half][:], pg_[:, 0:512], AF.Silu)
                    X.inc(i1, "ac")
                    X.mark(("psfree", pgb), "ac")
                    X.mark(("gsw", half), "ac")
                    X.wait_mark("vector", ("gsw", half))
                    X.wait_mark("vector", (f"pu{half}",))
                    i1 = None
                    if X.on("vector"):
                        i1 = eng.tensor_mul(h_sb[j][:, half * 512:(half + 1) * 512],
                                            gs_s[half][:], pu_[:, 0:512])
                    X.inc(i1, "dv")
                    X.mark(("psfree", pub), "dv")
                    X.mark(("gsfree", half), "dv")
                X.mark(("hw", j), "dv")
            for dt_ in range(D_TILES):
                X.mark(("xlnfree", dt_), "pe")

            for et in range(D_TILES):
                sl = wcnt[0] % 2
                qi = wcnt[0] % 3
                qe = ROT[qi]
                wcnt[0] += 1
                qi2 = (qi + 1) % 3
                qe2 = ROT[qi2]
                X.wait_mark(qe, ("wslab", sl))
                X.dma(qe, f"wsl{qi}{sl}", wslab[sl][:, 0:2752], wdT[et][:, 0:2752])
                X.wait_mark(qe2, ("wslab", sl))
                X.dma(qe2, f"wsu{qi2}{sl}", wslab[sl][:, 2752:5504],
                      wdT[et][:, 2752:5504])
                for half in (0, 1):
                    xb_ = half * 2 + et % 2
                    X.wait_mark("sync", ("xresfree", xb_))
                    X.dma("sync", f"xr{xb_}", xres[xb_][:],
                          x1T[et * 128:(et + 1) * 128, half * 512:(half + 1) * 512])
                X.wait_stream("tensor", f"wsl{qi}{sl}")
                X.wait_stream("tensor", f"wsu{qi2}{sl}")
                X.wait_mark("tensor", ("hw", IF_TILES - 1))
                for half, bank in [(0, PS[5]), (1, PS[4])]:
                    X.wait_mark("tensor", ("psfree", 5 - half))
                    i1 = None
                    if X.on("tensor"):
                        for kt in range(IF_TILES):
                            i1 = eng.matmul(bank[:, 0:512],
                                            wslab[sl][:, kt * 128:(kt + 1) * 128],
                                            h_sb[kt][:, half * 512:(half + 1) * 512],
                                            start=(kt == 0), stop=(kt == IF_TILES - 1))
                    X.inc(i1, "pe")
                    X.mark((f"pd{half}",), "pe")
                X.mark(("wslab", sl), "pe")
                xb2 = et % 2
                X.wait_stream("vector", "x2s")
                X.wait_mark("vector", ("x2bf", xb2))
                for half, bank, pb in [(0, PS[5], 5), (1, PS[4], 4)]:
                    xb_ = half * 2 + et % 2
                    X.wait_mark("vector", (f"pd{half}",))
                    X.wait_stream("vector", f"xr{xb_}")
                    i1 = None
                    if X.on("vector"):
                        i1 = eng.scalar_tensor_tensor(
                            x2t[xb2][:, half * 512:(half + 1) * 512],
                            bank[:, 0:512], 1.0, xres[xb_][:], ALU.mult, ALU.add)
                    X.inc(i1, "dv")
                    X.mark(("psfree", pb), "dv")
                    X.mark(("xresfree", xb_), "dv")
                X.mark(("x2w", et), "dv")
                X.wait_mark("sync", ("x2w", et))
                X.dma("sync", "x2s", x2T[et * 128:(et + 1) * 128, :], x2t[xb2][:])
                X.wait_mark("scalar", ("x2w", et))
                i1 = None
                if X.on("scalar"):
                    i1 = eng.activation(xln[et][:], x2t[xb2][:], AF.Copy, bias=0.0)
                X.inc(i1, "ac")
                X.mark(("x2b", et), "ac")
                X.mark(("x2bf", xb2), "ac")

            emit_ln_w(X, None)

            order = list(range(16)) + [o for hh in range(16) for o in (16 + hh, 32 + hh)]
            qcnt = [0]

            def emit_qkv_mm(ot):
                idx = qcnt[0]
                sl = idx % 2
                qi = idx % 3
                qe = ROT[qi]
                qb = idx % 2
                qcnt[0] += 1
                X.wait_mark(qe, ("w2slab", sl))
                X.dma(qe, f"w2{qi}{sl}", w2slab[sl][:], wqkvT[ot])
                X.wait_stream("tensor", f"w2{qi}{sl}")
                X.wait_mark("tensor", ("xlnw", D_TILES - 1))
                X.wait_mark("tensor", ("xlnw", 14))
                for half, bank in [(0, PS[5]), (1, PS[4])]:
                    X.wait_mark("tensor", ("psfree", 5 - half))
                    i1 = None
                    if X.on("tensor"):
                        for dt_ in range(D_TILES):
                            i1 = eng.matmul(bank[:, 0:512],
                                            w2slab[sl][:, dt_ * 128:(dt_ + 1) * 128],
                                            xln[dt_][:, half * 512:(half + 1) * 512],
                                            start=(dt_ == 0), stop=(dt_ == D_TILES - 1))
                    X.inc(i1, "pe")
                X.mark(("w2slab", sl), "pe")
                X.wait("vector", "pe")
                X.wait_mark("vector", ("qkvfree", qb))
                X.wait_stream("vector", f"qs{qb}")
                i1 = None
                if X.on("vector"):
                    eng.tensor_scalar(qkvt[qb][:, 0:512], PS[5][:, 0:512],
                                      bqkv_s[:, ot:ot + 1], None, ALU.add)
                    i1 = eng.tensor_scalar(qkvt[qb][:, 512:1024], PS[4][:, 0:512],
                                           bqkv_s[:, ot:ot + 1], None, ALU.add)
                X.inc(i1, "dv")
                X.mark(("psfree", 5), "dv")
                X.mark(("psfree", 4), "dv")
                X.mark(("qkvw", qb), "dv")

            def emit_qkv_post(ot, qb):
                if ot < 16:
                    X.wait_mark("sync", ("qkvw", qb))
                    X.dma("sync", f"qs{qb}", qT[ot * 128:(ot + 1) * 128, :], qkvt[qb][:])
                    X.mark(("qkvfree", qb), "dv")
                elif ot < 32:
                    hh = ot - 16
                    X.wait_mark("sync", ("qkvw", qb))
                    X.dma("sync", f"qs{qb}", kfT[hh * 128:(hh + 1) * 128, :], qkvt[qb][:])
                    for m in range(8):
                        tb = m % 2
                        X.wait_mark("tensor", ("qkvw", qb))
                        X.wait_mark("tensor", ("ptfree", tb))
                        i1 = None
                        if X.on("tensor"):
                            i1 = eng.transpose(PT[tb][:, 0:128],
                                               qkvt[qb][:, m * 128:(m + 1) * 128],
                                               identb_s[:])
                        X.inc(i1, "pe")
                        if m == 7:
                            X.mark(("qkvfree", qb), "pe")
                        X.wait("scalar", "pe")
                        X.wait_mark("scalar", ("ktokfree", m))
                        X.wait_stream("scalar", "kts")
                        i1 = None
                        if X.on("scalar"):
                            i1 = eng.copy(ktok_sb[m][:], PT[tb][:, 0:128])
                        X.inc(i1, "ac")
                        X.mark(("ptfree", tb), "ac")
                        X.mark(("ktokw", m), "ac")
                        X.wait_mark("sync", ("ktokw", m))
                        X.dma("sync", "kts",
                              ktokT[m * 128:(m + 1) * 128, hh * 128:(hh + 1) * 128],
                              ktok_sb[m][:])
                else:
                    hh = ot - 32
                    for m in range(8):
                        tb = m % 2
                        vb = m % 2
                        X.wait_mark("tensor", ("qkvw", qb))
                        X.wait_mark("tensor", ("ptfree", tb))
                        i1 = None
                        if X.on("tensor"):
                            i1 = eng.transpose(PT[tb][:, 0:128],
                                               qkvt[qb][:, m * 128:(m + 1) * 128],
                                               identb_s[:])
                        X.inc(i1, "pe")
                        if m == 7:
                            X.mark(("qkvfree", qb), "pe")
                        X.wait("scalar", "pe")
                        X.wait_mark("scalar", ("vtfree", vb))
                        X.wait_stream("scalar", f"vs{vb}")
                        i1 = None
                        if X.on("scalar"):
                            i1 = eng.copy(vt_s[vb][:], PT[tb][:, 0:128])
                        X.inc(i1, "ac")
                        X.mark(("ptfree", tb), "ac")
                        X.mark(("vtw", vb), "ac")
                        X.wait_mark("sync", ("vtw", vb))
                        X.dma("sync", f"vs{vb}",
                              vtokT[m * 128:(m + 1) * 128, hh * 128:(hh + 1) * 128],
                              vt_s[vb][:])
                        X.wait_mark("vector", ("kdfree",))
                        X.wait_mark("vector", ("ktokw", m))
                        i1 = None
                        if X.on("vector"):
                            i1 = eng.tensor_scalar(kd_s[:], ktok_sb[m][:],
                                                   dkcol_s[:, hh:hh + 1], None,
                                                   ALU.mult)
                        X.inc(i1, "dv")
                        X.mark(("kdw",), "dv")
                        X.mark(("ktokfree", m), "dv")
                        psd, psdb = (PS[0], 0) if m % 2 == 0 else (PS[1], 1)
                        X.wait_mark("tensor", ("kdw",))
                        X.wait_mark("tensor", ("vtw", vb))
                        X.wait_mark("tensor", ("psfree", psdb))
                        i1 = None
                        if X.on("tensor"):
                            i1 = eng.matmul(psd[:, 0:128], kd_s[:], vt_s[vb][:],
                                            start=True, stop=True)
                        X.inc(i1, "pe")
                        X.mark(("kdfree",), "pe")
                        X.mark(("vtfree", vb), "pe")
                        X.wait("vector", "pe")
                        X.wait_mark("vector", ("srunf",))
                        i1 = None
                        if X.on("vector"):
                            i1 = eng.scalar_tensor_tensor(
                                srun[:, hh * 128:(hh + 1) * 128],
                                srun[:, hh * 128:(hh + 1) * 128],
                                lam_c[hh], psd[:, 0:128], ALU.mult, ALU.add)
                        X.inc(i1, "dv")
                        X.mark(("srunf",), "dv")
                        X.mark(("psfree", psdb), "dv")

            for idx, ot in enumerate(order):
                emit_qkv_mm(ot)
                if idx >= 1:
                    emit_qkv_post(order[idx - 1], (idx - 1) % 2)
            emit_qkv_post(order[-1], (len(order) - 1) % 2)

            X.wait_mark("sync", ("srunf",))
            X.dma("sync", "tjs", TJ.rearrange("(h p) e -> p h e", h=16),
                  srun[:].rearrange("p (h e) -> p h e", h=16))
            return X

        @block.sync
        def _(eng):
            emit(Ctx("sync", eng, sems, spool))

        @block.vector
        def _(eng):
            emit(Ctx("vector", SerialEng(eng, auto["dv"]), sems, spool))

        @block.scalar
        def _(eng):
            emit(Ctx("scalar", SerialEng(eng, auto["ac"]), sems, spool))

        @block.tensor
        def _(eng):
            emit(Ctx("tensor", eng, sems, spool))

        @block.gpsimd
        def _(eng):
            X = emit(Ctx("gpsimd", SerialEng(eng, auto["gp"]), sems, spool))
            for name in X.streams:
                idx, cnt = X.streams[name]
                if cnt:
                    eng.wait_ge(spool[idx], cnt)

    return nc


def build_launch_c():
    nc = bass.Bass("TRN2", target_bir_lowering=False)

    x2T = nc.dram_tensor("x2T", [E, TPC], F32, kind="ExternalInput").ap()
    qTd = nc.dram_tensor("qT", [E, TPC], BF16, kind="ExternalInput").ap()
    kfTd = nc.dram_tensor("kfT", [E, TPC], BF16, kind="ExternalInput").ap()
    ktokTd = nc.dram_tensor("ktokT", [TPC, E], BF16, kind="ExternalInput").ap()
    vtokTd = nc.dram_tensor("vtokT", [TPC, E], BF16, kind="ExternalInput").ap()
    TJall = nc.dram_tensor("TJall", [NCORES * E, 128], F32, kind="ExternalInput").ap()
    wvec = nc.dram_tensor("wvec", [128, 128], F32, kind="ExternalInput").ap()
    woutbT = nc.dram_tensor("woutbT", [16, 128, D_TILES * 128], BF16, kind="ExternalInput").ap()
    bout = nc.dram_tensor("bout", [128, 16], F32, kind="ExternalInput").ap()
    dmaskTd = nc.dram_tensor("dmaskT", [16, 128, 128], BF16, kind="ExternalInput").ap()
    dqcd = nc.dram_tensor("dqc", [16, 128, 128], BF16, kind="ExternalInput").ap()
    dkcol = nc.dram_tensor("dkcol", [128, 16], F32, kind="ExternalInput").ap()
    wgT = nc.dram_tensor("wgTt", [IF_TILES, 128, E], BF16, kind="ExternalInput").ap()
    wuT = nc.dram_tensor("wuTt", [IF_TILES, 128, E], BF16, kind="ExternalInput").ap()
    wdT = nc.dram_tensor("wdTt", [D_TILES, 128, INTER_F], BF16, kind="ExternalInput").ap()
    epsvd = nc.dram_tensor("epsv", [128, 8], F32, kind="ExternalInput").ap()
    onescolbd = nc.dram_tensor("onescolb", [128, 1], BF16, kind="ExternalInput").ap()
    onesrowbd = nc.dram_tensor("onesrowb", [1, 128], BF16, kind="ExternalInput").ap()
    identbd = nc.dram_tensor("identb", [128, 128], BF16, kind="ExternalInput").ap()
    identfd = nc.dram_tensor("identf", [128, 128], F32, kind="ExternalInput").ap()

    outT = nc.dram_tensor("outFT", [E, TPC], F32, kind="ExternalOutput").ap()
    x3T = nc.dram_tensor("x3T", [E, TPC], F32, kind="Internal").ap()

    lam = _lam()
    lam_c = [float(v) for v in lam ** 128]

    from contextlib import ExitStack
    with ExitStack() as ctx:
        en = ctx.enter_context
        sb = lambda name, shape, dt: en(nc.sbuf_tensor(name, shape, dt))

        epsv_s = sb("epsv_s", [128, 8], F32)
        onescolb_s = sb("onescolb_s", [128, 1], BF16)
        onesrowb_s = sb("onesrowb_s", [1, 128], BF16)
        identb_s = sb("identb_s", [128, 128], BF16)
        identf_s = sb("identf_s", [128, 128], F32)
        bout_s = sb("bout_s", [128, 16], F32)
        dkcol_s = sb("dkcol_s", [128, 16], F32)
        wvec_s = sb("wvec_s", [128, 128], F32)
        dmask_s = sb("dmask_s", [128, 16 * 128], BF16)
        dqc_s = sb("dqc_s", [128, 16 * 128], BF16)

        xres = [sb(f"xres{i}", [128, 512], F32) for i in range(4)]
        xln = [sb(f"xln{i}", [128, TPC], BF16) for i in range(D_TILES)]
        x3t = [sb(f"x3t{i}", [128, TPC], F32) for i in range(1)]
        sqt = [sb(f"sqt{i}", [128, TPC], BF16) for i in range(2)]
        rowf = sb("rowf", [1, 3072], F32)
        rowb = sb("rowb", [1, 2048], BF16)
        rbc_s = sb("rbc_s", [128, TPC], BF16)
        mbc_s = sb("mbc_s", [128, TPC], BF16)
        h_sb = [sb(f"h{i}", [128, TPC], BF16) for i in range(IF_TILES)]
        gs_s = [sb(f"gs{i}", [128, 512], BF16) for i in range(2)]
        wslab = [sb(f"wslab{i}", [128, 5504], BF16) for i in range(2)]
        attn_t = h_sb
        srun = sb("srun", [128, 16 * 128], F32)
        sbt = [sb(f"sbt{e}", [128, 128], BF16) for e in range(2)]
        tjb = [sb(f"tjb{i}", [128, 512], F32) for i in range(1)]
        qk_t = [[sb(f"qk{e}{i}", [128, 128], BF16) for i in range(2)]
                for e in range(2)]
        kf_t = [[sb(f"kf{e}{i}", [128, 128], BF16) for i in range(2)]
                for e in range(2)]
        vt_t = [[sb(f"vt{e}{i}", [128, 128], BF16) for i in range(2)]
                for e in range(2)]
        kt_t = [[sb(f"kt{e}{i}", [128, 128], BF16) for i in range(2)]
                for e in range(2)]
        kd_s = [sb(f"kd_s{e}", [128, 128], BF16) for e in range(2)]
        msb = [sb(f"msb{e}", [128, 128], BF16) for e in range(2)]
        qdq = [sb(f"qdq{e}", [128, 128], BF16) for e in range(2)]
        x4t = [sb(f"x4t{i}", [128, TPC], F32) for i in range(1)]
        ot_s = [sb(f"ot{i}", [128, 128], F32) for i in range(1)]

        PS = [en(nc.psum_tensor(f"PS{i}", [128, 512], F32)) for i in range(6)]
        PT = [en(nc.psum_tensor(f"PT{i}", [128, 1024], BF16)) for i in range(2)]

        sems = {n: en(nc.semaphore("s_" + n)) for n in SEMNAMES}
        spool = [en(nc.semaphore(f"sp{i}")) for i in range(52)]
        auto = {n: en(nc.semaphore("a_" + n)) for n in ["dv", "ac", "gp"]}
        block = en(nc.Block())

        def emit(X: Ctx):
            eng = X.eng
            for dst, dsrc in [(epsv_s[:], epsvd), (onescolb_s[:], onescolbd),
                              (onesrowb_s[:], onesrowbd), (identb_s[:], identbd),
                              (identf_s[:], identfd), (bout_s[:], bout),
                              (dkcol_s[:], dkcol), (wvec_s[:], wvec),
                              (dmask_s[:].rearrange("p (h e) -> p h e", h=16),
                               dmaskTd.rearrange("h p e -> p h e")),
                              (dqc_s[:].rearrange("p (h e) -> p h e", h=16),
                               dqcd.rearrange("h p e -> p h e"))]:
                X.dma("sync", "c", dst, dsrc)
            for e_ in ["vector", "scalar", "tensor", "gpsimd"]:
                X.wait_stream(e_, "c")

            i1 = None
            if X.on("vector"):
                i1 = eng.memset(srun[:], 0.0)
            X.inc(i1, "dv")
            X.mark(("srun",), "dv")
            for hh0 in (0, 4, 8, 12):
                for j in range(NCORES):
                    tb = 0
                    X.wait_mark("sync", ("tjfree", tb))
                    X.dma("sync", f"tj{tb}",
                          tjb[tb][:].rearrange("p (h e) -> p h e", h=4),
                          TJall[j * E + hh0 * 128:j * E + (hh0 + 4) * 128, :]
                          .rearrange("(h p) e -> p h e", h=4))
                    X.wait_stream("vector", f"tj{tb}")
                    i1 = None
                    if X.on("vector"):
                        for h2 in range(4):
                            hh = hh0 + h2
                            i1 = eng.scalar_tensor_tensor(
                                srun[:, hh * 128:(hh + 1) * 128],
                                tjb[tb][:, h2 * 128:(h2 + 1) * 128],
                                wvec_s[:, j * 16 + hh:j * 16 + hh + 1],
                                srun[:, hh * 128:(hh + 1) * 128], ALU.mult, ALU.add)
                    X.inc(i1, "dv")
                    X.mark(("tjfree", tb), "dv")
                X.mark(("srunb", hh0 // 4), "dv")

            for qh in range(8):
                for e in range(2):
                    hh = 2 * qh + e
                    X.wait_mark("vector", ("srunb", qh // 2))
                    X.wait_mark("vector", ("sbtfree", e))
                    i1 = None
                    if X.on("vector"):
                        i1 = eng.tensor_copy(sbt[e][:],
                                             srun[:, hh * 128:(hh + 1) * 128])
                    X.inc(i1, "dv")
                    X.mark(("sbtw", e), "dv")
                for m in range(8):
                    for e in range(2):
                        hh = 2 * qh + e
                        mb = m % 2
                        bS, bS_id = (PS[0], 0) if e == 0 else (PS[3], 3)
                        bI, bI_id = (PS[1], 1) if e == 0 else (PS[4], 4)
                        bD, bD_id = (PS[2], 2) if e == 0 else (PS[5], 5)
                        tq = "sync" if e == 0 else "scalar"
                        for (nmk, buf, dsrc, rows, cols) in [
                                ("qk", qk_t[e][mb], qTd, hh, m),
                                ("kf", kf_t[e][mb], kfTd, hh, m)]:
                            X.wait_mark(tq, ("tilefree", nmk, e, mb))
                            X.dma(tq, f"t{nmk}{e}{mb}", buf[:],
                                  dsrc[rows * 128:(rows + 1) * 128,
                                       cols * 128:(cols + 1) * 128])
                        for (nmk, buf, dsrc) in [("vt", vt_t[e][mb], vtokTd),
                                                 ("kt", kt_t[e][mb], ktokTd)]:
                            X.wait_mark(tq, ("tilefree", nmk, e, mb))
                            X.dma(tq, f"t{nmk}{e}{mb}", buf[:],
                                  dsrc[m * 128:(m + 1) * 128,
                                       hh * 128:(hh + 1) * 128])

                        X.wait_stream("tensor", f"tqk{e}{mb}")
                        X.wait_stream("tensor", f"tkf{e}{mb}")
                        X.wait_mark("tensor", ("psfree", bS_id))
                        i1 = None
                        if X.on("tensor"):
                            i1 = eng.matmul(bS[:, 0:128], kf_t[e][mb][:],
                                            qk_t[e][mb][:], start=True, stop=True)
                        X.inc(i1, "pe")
                        sc_pe = X.c["pe"]
                        X.wait("vector", "pe", sc_pe)
                        X.wait_mark("vector", ("msbfree", e))
                        i1 = None
                        if X.on("vector"):
                            eng.tensor_mul(msb[e][:], bS[:, 0:128],
                                           dmask_s[:, hh * 128:(hh + 1) * 128])
                            i1 = eng.tensor_mul(qdq[e][:], qk_t[e][mb][:],
                                                dqc_s[:, hh * 128:(hh + 1) * 128])
                        X.inc(i1, "dv")
                        X.mark(("psfree", bS_id), "dv")
                        X.mark(("msbw", e), "dv")
                        X.wait_mark("tensor", ("msbw", e))
                        X.wait_stream("tensor", f"tvt{e}{mb}")
                        X.wait_mark("tensor", ("psfree", bI_id))
                        X.wait_mark("tensor", ("sbtw", e))
                        i1 = None
                        if X.on("tensor"):
                            eng.matmul(bI[:, 0:128], vt_t[e][mb][:], msb[e][:],
                                       start=True, stop=False)
                            i1 = eng.matmul(bI[:, 0:128], sbt[e][:], qdq[e][:],
                                            start=False, stop=True)
                        X.inc(i1, "pe")
                        X.mark(("msbfree", e), "pe")
                        yb_pe = X.c["pe"]
                        X.wait("scalar", "pe", yb_pe)
                        i1 = None
                        if X.on("scalar"):
                            i1 = eng.activation(attn_t[hh][:, m * 128:(m + 1) * 128],
                                                bI[:, 0:128], AF.Copy, bias=0.0)
                        X.inc(i1, "ac")
                        X.mark(("psfree", bI_id), "ac")
                        X.mark(("attnw", hh), "ac")
                        X.wait_mark("vector", ("kdfree", e))
                        X.wait_stream("vector", f"tkt{e}{mb}")
                        i1 = None
                        if X.on("vector"):
                            i1 = eng.tensor_scalar(kd_s[e][:], kt_t[e][mb][:],
                                                   dkcol_s[:, hh:hh + 1], None,
                                                   ALU.mult)
                        X.inc(i1, "dv")
                        X.mark(("kdw", e), "dv")
                        X.wait_mark("tensor", ("kdw", e))
                        X.wait_mark("tensor", ("psfree", bD_id))
                        i1 = None
                        if X.on("tensor"):
                            i1 = eng.matmul(bD[:, 0:128], kd_s[e][:],
                                            vt_t[e][mb][:], start=True, stop=True)
                        X.inc(i1, "pe")
                        X.mark(("kdfree", e), "pe")
                        X.mark(("tilefree", "qk", e, mb), "pe")
                        X.mark(("tilefree", "kf", e, mb), "pe")
                        X.mark(("tilefree", "vt", e, mb), "pe")
                        X.mark(("tilefree", "kt", e, mb), "pe")
                        ds_pe = X.c["pe"]
                        X.wait("vector", "pe", ds_pe)
                        X.wait_mark("vector", ("sbtfree", e))
                        i1 = None
                        if X.on("vector"):
                            if m < 7:
                                eng.scalar_tensor_tensor(
                                    srun[:, hh * 128:(hh + 1) * 128],
                                    srun[:, hh * 128:(hh + 1) * 128], lam_c[hh],
                                    bD[:, 0:128], ALU.mult, ALU.add)
                                i1 = eng.tensor_copy(
                                    sbt[e][:], srun[:, hh * 128:(hh + 1) * 128])
                            else:
                                i1 = eng.scalar_tensor_tensor(
                                    srun[:, hh * 128:(hh + 1) * 128],
                                    srun[:, hh * 128:(hh + 1) * 128], lam_c[hh],
                                    bD[:, 0:128], ALU.mult, ALU.add)
                        X.inc(i1, "dv")
                        X.mark(("psfree", bD_id), "dv")
                        X.mark(("srun",), "dv")
                        if m < 7:
                            X.mark(("sbtw", e), "dv")
                        else:
                            X.mark(("sbtfree", e), "dv")

            wcnt = [0]
            ROT = ("sync", "scalar", "gpsimd")
            for et in range(D_TILES):
                sl = wcnt[0] % 2
                qi = wcnt[0] % 3
                qe = ROT[qi]
                wcnt[0] += 1
                X.wait_mark(qe, ("wslab", sl))
                X.dma(qe, f"wo{qi}{sl}", wslab[sl][:, 0:2048], woutbT[et])
                for half in (0, 1):
                    xb_ = half * 2 + et % 2
                    X.wait_mark("sync", ("xresfree", xb_))
                    X.dma("sync", f"xr{xb_}", xres[xb_][:],
                          x2T[et * 128:(et + 1) * 128, half * 512:(half + 1) * 512])
                X.wait_stream("tensor", f"wo{qi}{sl}")
                if et == 0:
                    for kt in range(D_TILES):
                        X.wait_mark("tensor", ("attnw", kt))
                for half, bank in [(0, PS[5]), (1, PS[4])]:
                    X.wait_mark("tensor", ("psfree", 5 - half))
                    i1 = None
                    if X.on("tensor"):
                        for kt in range(D_TILES):
                            i1 = eng.matmul(bank[:, 0:512],
                                            wslab[sl][:, kt * 128:(kt + 1) * 128],
                                            attn_t[kt][:, half * 512:(half + 1) * 512],
                                            start=(kt == 0), stop=(kt == D_TILES - 1))
                    X.inc(i1, "pe")
                    X.mark((f"po{half}",), "pe")
                X.mark(("wslab", sl), "pe")
                xb2 = 0
                X.wait_stream("vector", "x3s")
                X.wait_mark("vector", ("x3bf", xb2))
                for half, bank, pb in [(0, PS[5], 5), (1, PS[4], 4)]:
                    xb_ = half * 2 + et % 2
                    X.wait_mark("vector", (f"po{half}",))
                    X.wait_stream("vector", f"xr{xb_}")
                    i1 = None
                    if X.on("vector"):
                        i1 = eng.scalar_tensor_tensor(
                            x3t[xb2][:, half * 512:(half + 1) * 512],
                            bank[:, 0:512], bout_s[:, et:et + 1],
                            xres[xb_][:], ALU.add, ALU.add)
                    X.inc(i1, "dv")
                    X.mark(("psfree", pb), "dv")
                    X.mark(("xresfree", xb_), "dv")
                X.mark(("x3w", et), "dv")
                X.wait_mark("sync", ("x3w", et))
                X.dma("sync", "x3s", x3T[et * 128:(et + 1) * 128, :], x3t[xb2][:])
                X.wait_mark("scalar", ("x3w", et))
                i1 = None
                if X.on("scalar"):
                    i1 = eng.activation(xln[et][:], x3t[xb2][:], AF.Copy, bias=0.0)
                X.inc(i1, "ac")
                X.mark(("x2b", et), "ac")
                X.mark(("x3bf", xb2), "ac")

            emit_ln_w(X, None)

            for j in range(IF_TILES):
                sl = wcnt[0] % 2
                qi = wcnt[0] % 3
                qe = ROT[qi]
                wcnt[0] += 1
                qi2 = (qi + 1) % 3
                qe2 = ROT[qi2]
                X.wait_mark(qe, ("wslab", sl))
                X.dma(qe, f"wsl{qi}{sl}", wslab[sl][:, 0:2048], wgT[j])
                X.wait_mark(qe2, ("wslab", sl))
                X.dma(qe2, f"wsu{qi2}{sl}", wslab[sl][:, 2048:4096], wuT[j])
                X.wait_stream("tensor", f"wsl{qi}{sl}")
                X.wait_stream("tensor", f"wsu{qi2}{sl}")
                X.wait_mark("tensor", ("xlnw", D_TILES - 1))
                X.wait_mark("tensor", ("xlnw", 14))
                for (grp, bank, bid, wofs, h0) in [
                        ("pg0", PS[5], 5, 0, 0), ("pu0", PS[3], 3, 2048, 0),
                        ("pg1", PS[4], 4, 0, 512), ("pu1", PS[2], 2, 2048, 512)]:
                    X.wait_mark("tensor", ("psfree", bid))
                    i1 = None
                    if X.on("tensor"):
                        for dt_ in range(D_TILES):
                            i1 = eng.matmul(
                                bank[:, 0:512],
                                wslab[sl][:, wofs + dt_ * 128:wofs + (dt_ + 1) * 128],
                                xln[dt_][:, h0:h0 + 512],
                                start=(dt_ == 0), stop=(dt_ == D_TILES - 1))
                    X.inc(i1, "pe")
                    X.mark((grp,), "pe")
                X.mark(("wslab", sl), "pe")
                for half, pg_, pu_, pgb, pub in [(0, PS[5], PS[3], 5, 3),
                                                 (1, PS[4], PS[2], 4, 2)]:
                    X.wait_mark("scalar", (f"pg{half}",))
                    X.wait_mark("scalar", ("gsfree", half))
                    i1 = None
                    if X.on("scalar"):
                        i1 = eng.activation(gs_s[half][:], pg_[:, 0:512], AF.Silu)
                    X.inc(i1, "ac")
                    X.mark(("psfree", pgb), "ac")
                    X.mark(("gsw", half), "ac")
                    X.wait_mark("vector", ("gsw", half))
                    X.wait_mark("vector", (f"pu{half}",))
                    i1 = None
                    if X.on("vector"):
                        i1 = eng.tensor_mul(h_sb[j][:, half * 512:(half + 1) * 512],
                                            gs_s[half][:], pu_[:, 0:512])
                    X.inc(i1, "dv")
                    X.mark(("psfree", pub), "dv")
                    X.mark(("gsfree", half), "dv")
                X.mark(("hw", j), "dv")

            for et in range(D_TILES):
                sl = wcnt[0] % 2
                qi = wcnt[0] % 3
                qe = ROT[qi]
                wcnt[0] += 1
                qi2 = (qi + 1) % 3
                qe2 = ROT[qi2]
                X.wait_mark(qe, ("wslab", sl))
                X.dma(qe, f"wsl{qi}{sl}", wslab[sl][:, 0:2752], wdT[et][:, 0:2752])
                X.wait_mark(qe2, ("wslab", sl))
                X.dma(qe2, f"wsu{qi2}{sl}", wslab[sl][:, 2752:5504],
                      wdT[et][:, 2752:5504])
                for half in (0, 1):
                    xb_ = half * 2 + et % 2
                    X.wait_mark("sync", ("xresfree", xb_))
                    if et == 0 and half == 0:
                        X.wait_stream("sync", "x3s")
                    X.dma("sync", f"xr{xb_}", xres[xb_][:],
                          x3T[et * 128:(et + 1) * 128, half * 512:(half + 1) * 512])
                X.wait_stream("tensor", f"wsl{qi}{sl}")
                X.wait_stream("tensor", f"wsu{qi2}{sl}")
                X.wait_mark("tensor", ("hw", IF_TILES - 1))
                for half, bank in [(0, PS[5]), (1, PS[4])]:
                    X.wait_mark("tensor", ("psfree", 5 - half))
                    i1 = None
                    if X.on("tensor"):
                        for kt in range(IF_TILES):
                            i1 = eng.matmul(bank[:, 0:512],
                                            wslab[sl][:, kt * 128:(kt + 1) * 128],
                                            h_sb[kt][:, half * 512:(half + 1) * 512],
                                            start=(kt == 0), stop=(kt == IF_TILES - 1))
                    X.inc(i1, "pe")
                    X.mark((f"pd{half}",), "pe")
                X.mark(("wslab", sl), "pe")
                xb2 = 0
                X.wait_stream("vector", "os")
                for half, bank, pb in [(0, PS[5], 5), (1, PS[4], 4)]:
                    xb_ = half * 2 + et % 2
                    X.wait_mark("vector", (f"pd{half}",))
                    X.wait_mark("vector", ("x4free", xb2))
                    X.wait_stream("vector", f"xr{xb_}")
                    i1 = None
                    if X.on("vector"):
                        i1 = eng.scalar_tensor_tensor(
                            x4t[xb2][:, half * 512:(half + 1) * 512],
                            bank[:, 0:512], 1.0, xres[xb_][:], ALU.mult, ALU.add)
                    X.inc(i1, "dv")
                    X.mark(("psfree", pb), "dv")
                    X.mark(("xresfree", xb_), "dv")
                X.mark(("x4w", xb2), "dv")
                X.wait_mark("sync", ("x4w", xb2))
                X.dma("sync", "os", outT[et * 128:(et + 1) * 128, :], x4t[xb2][:])
                X.mark(("x4free", xb2), "dv")
            X.wait_stream("vector", "os")
            return X

        def emit_ln_w(X, src):
            eng = X.eng
            for dt_ in range(D_TILES):
                bb = dt_ % 2
                X.wait_mark("scalar", ("x2b", dt_))
                X.wait_mark("scalar", ("sqfree", bb))
                i1 = None
                if X.on("scalar"):
                    i1 = eng.activation(sqt[bb][:], xln[dt_][:], AF.Square)
                X.inc(i1, "ac")
                X.mark(("sqw", bb), "ac")
                X.wait_mark("tensor", ("sqw", bb))
                if dt_ == 0:
                    for b_ in range(4):
                        X.wait_mark("tensor", ("psfree", b_))
                i1 = None
                if X.on("tensor"):
                    eng.matmul(PS[0][0:1, 0:512], onescolb_s[:], xln[dt_][:, 0:512],
                               start=(dt_ == 0), stop=(dt_ == D_TILES - 1))
                    eng.matmul(PS[1][0:1, 0:512], onescolb_s[:], xln[dt_][:, 512:1024],
                               start=(dt_ == 0), stop=(dt_ == D_TILES - 1))
                    eng.matmul(PS[2][0:1, 0:512], onescolb_s[:], sqt[bb][:, 0:512],
                               start=(dt_ == 0), stop=(dt_ == D_TILES - 1))
                    i1 = eng.matmul(PS[3][0:1, 0:512], onescolb_s[:],
                                    sqt[bb][:, 512:1024],
                                    start=(dt_ == 0), stop=(dt_ == D_TILES - 1))
                X.inc(i1, "pe")
                X.mark(("sqfree", bb), "pe")
                X.mark(("xstat", dt_), "pe")
            X.wait("vector", "pe")
            i1 = None
            if X.on("vector"):
                eng.tensor_scalar(rowf[0:1, 0:512], PS[0][0:1, 0:512], 1.0 / E,
                                  None, ALU.mult)
                eng.tensor_scalar(rowf[0:1, 512:1024], PS[1][0:1, 0:512], 1.0 / E,
                                  None, ALU.mult)
                eng.tensor_scalar(rowf[0:1, 1024:1536], PS[2][0:1, 0:512], 1.0 / E,
                                  None, ALU.mult)
                eng.tensor_scalar(rowf[0:1, 1536:2048], PS[3][0:1, 0:512], 1.0 / E,
                                  None, ALU.mult)
                eng.tensor_mul(rowf[0:1, 2048:3072], rowf[0:1, 0:1024],
                               rowf[0:1, 0:1024])
                i1 = eng.tensor_sub(rowf[0:1, 2048:3072], rowf[0:1, 1024:2048],
                                    rowf[0:1, 2048:3072])
            X.inc(i1, "dv")
            for b_ in range(4):
                X.mark(("psfree", b_), "dv")
            X.wait("scalar", "dv")
            i1 = None
            if X.on("scalar"):
                eng.activation(rowf[0:1, 2048:3072], rowf[0:1, 2048:3072], AF.Ln,
                               bias=epsv_s[0:1, 0:1])
                i1 = eng.activation(rowf[0:1, 2048:3072], rowf[0:1, 2048:3072],
                                    AF.Exp, scale=-0.5)
            X.inc(i1, "ac")
            X.wait("vector", "ac")
            i1 = None
            if X.on("vector"):
                eng.tensor_copy(rowb[0:1, 0:1024], rowf[0:1, 2048:3072])
                eng.tensor_mul(rowf[0:1, 0:1024], rowf[0:1, 0:1024],
                               rowf[0:1, 2048:3072])
                i1 = eng.tensor_copy(rowb[0:1, 1024:2048], rowf[0:1, 0:1024])
            X.inc(i1, "dv")
            X.wait("tensor", "dv")
            for b_ in range(4):
                X.wait_mark("tensor", ("psfree", b_))
            i1 = None
            if X.on("tensor"):
                eng.matmul(PS[0][:, 0:512], onesrowb_s[:], rowb[0:1, 0:512],
                           start=True, stop=True)
                eng.matmul(PS[1][:, 0:512], onesrowb_s[:], rowb[0:1, 512:1024],
                           start=True, stop=True)
                eng.matmul(PS[2][:, 0:512], onesrowb_s[:], rowb[0:1, 1024:1536],
                           start=True, stop=True)
                i1 = eng.matmul(PS[3][:, 0:512], onesrowb_s[:], rowb[0:1, 1536:2048],
                                start=True, stop=True)
            X.inc(i1, "pe")
            X.wait("scalar", "pe")
            i1 = None
            if X.on("scalar"):
                eng.copy(rbc_s[:, 0:512], PS[0][:, 0:512])
                eng.copy(rbc_s[:, 512:1024], PS[1][:, 0:512])
                eng.copy(mbc_s[:, 0:512], PS[2][:, 0:512])
                i1 = eng.copy(mbc_s[:, 512:1024], PS[3][:, 0:512])
            X.inc(i1, "ac")
            for b_ in range(4):
                X.mark(("psfree", b_), "ac")
            bc_ac = X.c["ac"]
            for dt_ in range(D_TILES):
                ae = "vector" if dt_ % 3 != 2 else "gpsimd"
                asem = "dv" if ae == "vector" else "gp"
                X.wait(ae, "ac", bc_ac)
                X.wait_mark(ae, ("xstat", dt_))
                i1 = None
                if X.on(ae):
                    eng.tensor_mul(xln[dt_][:], xln[dt_][:], rbc_s[:])
                    i1 = eng.tensor_sub(xln[dt_][:], xln[dt_][:], mbc_s[:])
                X.inc(i1, asem)
                X.mark(("xlnw", dt_), asem)

        @block.sync
        def _(eng):
            emit(Ctx("sync", eng, sems, spool))

        @block.vector
        def _(eng):
            emit(Ctx("vector", SerialEng(eng, auto["dv"]), sems, spool))

        @block.scalar
        def _(eng):
            emit(Ctx("scalar", SerialEng(eng, auto["ac"]), sems, spool))

        @block.tensor
        def _(eng):
            emit(Ctx("tensor", eng, sems, spool))

        @block.gpsimd
        def _(eng):
            X = emit(Ctx("gpsimd", SerialEng(eng, auto["gp"]), sems, spool))
            for name in X.streams:
                idx, cnt = X.streams[name]
                if cnt:
                    eng.wait_ge(spool[idx], cnt)

    return nc



def _prep_bc_shared(inputs):
    w_qkv = np.asarray(inputs["L0_attn_b_w_qkv"], np.float32)
    b_qkv = np.asarray(inputs["L0_attn_b_b_qkv"], np.float32)
    w_out = np.asarray(inputs["L0_attn_b_w_out"], np.float32)
    b_out = np.asarray(inputs["L0_attn_b_b_out"], np.float32)
    lam = _lam()
    l_ = np.arange(128)
    dkcol = np.ascontiguousarray((lam[None, :] ** (127 - l_[:, None])).astype(np.float32))
    dmaskT = np.zeros((16, 128, 128), BF)
    for hh in range(16):
        diff = l_[None, :] - l_[:, None]
        dmaskT[hh] = np.where(diff >= 0, lam[hh] ** np.maximum(diff, 0), 0.0).astype(BF)
    dqc = np.zeros((16, 128, 128), BF)
    for hh in range(16):
        dqc[hh] = np.tile((lam[hh] ** (l_ + 1.0)).astype(BF)[None, :], (128, 1))
    shared_b = {
        "wqkvT": tile_lhsT(np.ascontiguousarray(w_qkv.T), D_TILES).astype(BF),
        "bqkv": np.ascontiguousarray(b_qkv.reshape(48, 128).T),
        "dkcol": dkcol,
        "epsv": np.tile(np.array([[LN_EPS, RMS_EPS, 1.0, 0, 0, 0, 0, 0]], np.float32), (128, 1)),
        "onescolb": np.ones((128, 1), BF),
        "onesrowb": np.ones((1, 128), BF),
        "identb": np.eye(128, dtype=BF),
    }
    shared_c = {
        "woutbT": tile_lhsT(np.ascontiguousarray(w_out.T), D_TILES).astype(BF),
        "bout": np.ascontiguousarray(b_out.reshape(16, 128).T),
        "dmaskT": dmaskT,
        "dqc": dqc,
        "dkcol": dkcol,
        "epsv": shared_b["epsv"],
        "onescolb": shared_b["onescolb"],
        "onesrowb": shared_b["onesrowb"],
        "identb": shared_b["identb"],
        "identf": np.eye(128, dtype=np.float32),
    }
    return shared_b, shared_c


def _wvec_for_core(c):
    lam = _lam()
    lam_c8 = lam ** (128 * 8)
    w = np.zeros((128, 8, 16), np.float64)
    b = c // 4
    for j in range(NCORES):
        if j // 4 == b and j < c:
            w[:, j, :] = lam_c8 ** (c - 1 - j)
    return np.ascontiguousarray(w.reshape(128, 128).astype(np.float32))


def kernel(**inputs):
    per_core_a = prep_launch_a_inputs(inputs)
    nc_a = build_launch_a()
    res_a = run_bass_kernel_spmd(nc_a, per_core_a, core_ids=list(range(NCORES)))

    wgT = np.concatenate([np.asarray(res_a.results[c]["wg_sh"]) for c in range(NCORES)], axis=0)
    wuT = np.concatenate([np.asarray(res_a.results[c]["wu_sh"]) for c in range(NCORES)], axis=0)
    wdT = np.concatenate([np.asarray(res_a.results[c]["wd_sh"]) for c in range(NCORES)], axis=0)

    wgTt = np.ascontiguousarray(
        wgT.reshape(D_TILES, 128, IF_TILES, 128).transpose(2, 1, 0, 3)
        .reshape(IF_TILES, 128, E))
    wuTt = np.ascontiguousarray(
        wuT.reshape(D_TILES, 128, IF_TILES, 128).transpose(2, 1, 0, 3)
        .reshape(IF_TILES, 128, E))
    wdTt = np.ascontiguousarray(
        wdT.reshape(IF_TILES, 128, D_TILES, 128).transpose(2, 1, 0, 3)
        .reshape(D_TILES, 128, INTER_F))

    shared_b, shared_c = _prep_bc_shared(inputs)
    maps_b = []
    for c in range(NCORES):
        m = dict(shared_b)
        m["x1T"] = np.asarray(res_a.results[c]["x1T"])
        m["x1bT"] = np.asarray(res_a.results[c]["x1bT"])
        m["wgTt"] = wgTt
        m["wuTt"] = wuTt
        m["wdTt"] = wdTt
        maps_b.append(m)
    nc_b = build_launch_b()
    res_b = run_bass_kernel_spmd(nc_b, maps_b, core_ids=list(range(NCORES)))

    TJall = np.concatenate([np.asarray(res_b.results[c]["TJ"]) for c in range(NCORES)], axis=0)
    maps_c = []
    for c in range(NCORES):
        m = dict(shared_c)
        for k in ["x2T", "qT", "kfT", "ktokT", "vtokT"]:
            m[k] = np.asarray(res_b.results[c][k])
        m["TJall"] = TJall
        m["wvec"] = _wvec_for_core(c)
        m["wgTt"] = wgTt
        m["wuTt"] = wuTt
        m["wdTt"] = wdTt
        maps_c.append(m)
    nc_c = build_launch_c()
    res_c = run_bass_kernel_spmd(nc_c, maps_c, core_ids=list(range(NCORES)))

    out = np.zeros((2, 4096, E), np.float32)
    for c in range(NCORES):
        b, q = divmod(c, 4)
        out[b, q * TPC:(q + 1) * TPC, :] = np.asarray(res_c.results[c]["outFT"]).T
    return out
